# version 12
# speedup vs baseline: 1.3545x; 1.0215x over previous
import numpy as np
import ml_dtypes
from contextlib import ExitStack
import concourse.bass as bass
import concourse.mybir as mybir
from concourse.bass_utils import run_bass_kernel_spmd

F32 = mybir.dt.float32
BF16 = mybir.dt.bfloat16
I32 = mybir.dt.int32
AF = mybir.ActivationFunctionType
ALU = mybir.AluOpType
AX = mybir.AxisListType

NCORE = 8
B, N, D = 4, 8192, 1024
H = 16
DHID = 4096
TOK = (B * N) // NCORE
NT1 = TOK // 128
S_HEAD = float((64 ** 0.5) ** (-0.25))
EPS = 1e-5
MAGIC = 12582912.0
GW = 510
NG = -(-TOK // GW)
GOUT = [min(GW, TOK - g * GW) for g in range(NG)]
XQW = 4608
W1RES = 12

DBG = False
STAGE = 7
WLIM = NG if True else 0
TLIM = 10**9


def _ceil(a, b):
    return -(-a // b)


def _a_sched():
    out = [list(range(0, 6))]
    nxt = 6
    for g in range(NG):
        need = min(NT1, _ceil(GW * (g + 1) + 2, 128) + 2)
        hi = min(max(nxt, need), NT1)
        if g == NG - 1:
            hi = NT1
        out.append(list(range(nxt, hi)))
        nxt = hi
    return out


A_SCHED = _a_sched()




_DT_SZ = {"dt.float32": 4, "dt.int32": 4, "dt.uint32": 4,
          "dt.bfloat16": 2, "dt.float16": 2, "dt.uint16": 2,
          "dt.int16": 2, "dt.uint8": 1, "dt.int8": 1}


def _apint(ap):
    if ap is None:
        return None
    try:
        sz = _DT_SZ.get(str(ap.dtype), 4)
        off = int(ap.offset)
        ext = 1
        for s, n in ap.ap:
            ext += int(s) * (int(n) - 1)
        return (ap.name, off * sz, (off + ext) * sz)
    except Exception:
        try:
            return (ap.name, 0, 1 << 62)
        except Exception:
            return None


class _WriteLog:

    def __init__(self):
        self._w = {}

    def clear(self):
        self._w.clear()

    def add(self, iv):
        if iv is not None:
            self._w.setdefault(iv[0], []).append((iv[1], iv[2]))

    def hits(self, iv):
        if iv is None:
            return False
        for lo, hi in self._w.get(iv[0], ()):
            if iv[1] < hi and lo < iv[2]:
                return True
        return False


class _HazardLog:

    def __init__(self):
        self._w = _WriteLog()
        self._r = _WriteLog()

    def clear(self):
        self._w.clear()
        self._r.clear()

    def hazard(self, rds, wrs):
        return (any(self._w.hits(iv) for iv in rds)
                or any(self._w.hits(iv) for iv in wrs)
                or any(self._r.hits(iv) for iv in wrs))

    def record(self, rds, wrs):
        for iv in rds:
            self._r.add(iv)
        for iv in wrs:
            self._w.add(iv)


class _ScalGuard:

    def __init__(self, sc):
        self._s = sc
        self._recent = _HazardLog()

    def wait_ge(self, s, v):
        return self._s.wait_ge(s, v)

    def drain(self, *a, **k):
        self._recent.clear()
        return self._s.drain(*a, **k)

    def nop(self):
        return self._s.nop()

    def dma_start(self, **kw):
        return self._s.dma_start(**kw)

    def activation(self, out, in_, func, bias=0.0, scale=1.0, **kw):
        rds = [_apint(in_)]
        if isinstance(bias, bass.AP):
            rds.append(_apint(bias))
        if isinstance(scale, bass.AP):
            rds.append(_apint(scale))
        wrs = [_apint(out)]
        ao = kw.get("accum_out")
        if ao is not None:
            wrs.append(_apint(ao))
        if self._recent.hazard(rds, wrs):
            self._s.drain()
            self._recent.clear()
        it = self._s.activation(out, in_, func, bias=bias, scale=scale, **kw)
        self._recent.record(rds, wrs)
        return it


class _VecGuard:

    def __init__(self, vec):
        self._v = vec
        self._recent = _HazardLog()

    def _emit(self, fn, outs, ins, *args, **kwargs):
        rds = [_apint(a) for a in ins if a is not None]
        wrs = [_apint(a) for a in outs if a is not None]
        if self._recent.hazard(rds, wrs):
            self._v.drain()
            self._recent.clear()
        it = fn(*args, **kwargs)
        self._recent.record(rds, wrs)
        return it

    def wait_ge(self, s, v):
        return self._v.wait_ge(s, v)

    def drain(self):
        self._recent.clear()
        return self._v.drain()

    def memset(self, ap, c):
        return self._emit(self._v.memset, [ap], [], ap, c)

    def tensor_copy(self, out, in_):
        return self._emit(self._v.tensor_copy, [out], [in_], out, in_)

    def tensor_tensor(self, out, a, b, op):
        return self._emit(self._v.tensor_tensor, [out], [a, b], out, a, b, op)

    def tensor_add(self, out, a, b):
        return self.tensor_tensor(out, a, b, ALU.add)

    def tensor_sub(self, out, a, b):
        return self.tensor_tensor(out, a, b, ALU.subtract)

    def tensor_mul(self, out, a, b):
        return self.tensor_tensor(out, a, b, ALU.mult)

    def tensor_max(self, out, a, b):
        return self.tensor_tensor(out, a, b, ALU.max)

    def tensor_scalar(self, out, in0, s1, s2, op0, op1=ALU.bypass, **kw):
        ins = [in0]
        if isinstance(s1, bass.AP):
            ins.append(s1)
        if isinstance(s2, bass.AP):
            ins.append(s2)
        ao = kw.get("accum_out")
        outs = [out] + ([ao] if ao is not None else [])
        return self._emit(self._v.tensor_scalar, outs, ins,
                          out, in0, s1, s2, op0, op1, **kw)

    def tensor_single_scalar(self, out, in_, s, op):
        ins = [in_] + ([s] if isinstance(s, bass.AP) else [])
        return self._emit(self._v.tensor_single_scalar, [out], ins,
                          out, in_, s, op)

    def scalar_tensor_tensor(self, out, in0, s, in1, op0, op1, **kw):
        ins = [in0, in1] + ([s] if isinstance(s, bass.AP) else [])
        ao = kw.get("accum_out")
        outs = [out] + ([ao] if ao is not None else [])
        return self._emit(self._v.scalar_tensor_tensor, outs, ins,
                          out, in0, s, in1, op0, op1, **kw)

    def reduce_sum(self, out, in_, **kw):
        return self._emit(self._v.reduce_sum, [out], [in_], out, in_, **kw)

    def reduce_max(self, out, in_, **kw):
        return self._emit(self._v.reduce_max, [out], [in_], out, in_, **kw)

    def reciprocal(self, out, in_):
        return self._emit(self._v.reciprocal, [out], [in_], out, in_)


_MISSING = set()


class StopEmission(Exception):
    pass


def _stage_guard(fn, epilogue=None, fin=None):
    def wrapped(eng):
        try:
            fn(eng)
        except StopEmission:
            pass
        if fin is not None:
            fin(eng)
        if epilogue is not None:
            epilogue(eng)
    return wrapped


def build_nc(tick, plan):
    NT1B = min(NT1, TLIM)
    """Build the SPMD program.  plan=True: collect semaphore milestones into
    tick (waits on unknown events are skipped).  plan=False: emit with real
    waits from a completed tick."""
    nc = bass.Bass("TRN2", target_bir_lowering=False)

    x_in = nc.dram_tensor("x_in", [TOK, D], F32, kind="ExternalInput")
    x_halo = nc.dram_tensor("x_halo", [1, D], F32, kind="ExternalInput")
    Wq = nc.dram_tensor("Wq", [D, D], F32, kind="ExternalInput")
    Wv = nc.dram_tensor("Wv", [D, D], F32, kind="ExternalInput")
    Wo = nc.dram_tensor("Wo", [D, D], F32, kind="ExternalInput")
    bq_d = nc.dram_tensor("bq", [1, D], F32, kind="ExternalInput")
    bv_d = nc.dram_tensor("bv", [1, D], F32, kind="ExternalInput")
    bo_d = nc.dram_tensor("bo", [1, D], F32, kind="ExternalInput")
    w1T = nc.dram_tensor("w1T", [D, DHID], F32, kind="ExternalInput")
    w2T = nc.dram_tensor("w2T", [DHID, D], F32, kind="ExternalInput")
    cw_d = nc.dram_tensor("cw", [3, DHID], F32, kind="ExternalInput")
    cb_d = nc.dram_tensor("cb", [1, DHID], F32, kind="ExternalInput")
    par_d = nc.dram_tensor("par", [1, 2], F32, kind="ExternalInput")

    y_out = nc.dram_tensor("y_out", [TOK, D], F32, kind="ExternalOutput")

    xln_hbm = nc.dram_tensor("xln_hbm", [TOK, D], BF16)
    q_hbm = nc.dram_tensor("q_hbm", [TOK, D], BF16)
    x1_hbm = nc.dram_tensor("x1_hbm", [TOK, D], F32)
    xq_hbm = nc.dram_tensor("xq_hbm", [XQW, D], BF16)
    wq1_hbm = nc.dram_tensor("wq1_hbm", [32 - W1RES, 128, 8, 128], BF16)
    wq2_hbm = nc.dram_tensor("wq2_hbm", [32, 128, D], BF16)
    cc_in = nc.dram_tensor("cc_in", [67, D], F32)
    cc_out = nc.dram_tensor("cc_out", [67, D], F32)
    hrow_hbm = nc.dram_tensor("hrow_hbm", [1, D], F32)
    ident_d = nc.inline_tensor(np.eye(128, dtype=np.float32), name="ident")
    if DBG:
        dbg_q = nc.dram_tensor("dbg_q", [TOK, D], BF16, kind="ExternalOutput")
        dbg_xln = nc.dram_tensor("dbg_xln", [TOK, D], BF16,
                                 kind="ExternalOutput")
        dbg_wq2 = nc.dram_tensor("dbg_wq2", [32, 128, D], BF16,
                                 kind="ExternalOutput")
        dbg_st = nc.dram_tensor("dbg_st", [128, 24], F32,
                                 kind="ExternalOutput")
        dbg_cc = nc.dram_tensor("dbg_cc", [67, D], F32, kind="ExternalOutput")
        dbg_M = nc.dram_tensor("dbg_M", [128, 8192], BF16,
                               kind="ExternalOutput")
        dbg_x1 = nc.dram_tensor("dbg_x1", [TOK, D], F32, kind="ExternalOutput")
        dbg_xqt = nc.dram_tensor("dbg_xqt", [XQW, D], BF16,
                                 kind="ExternalOutput")
        dbg_h = nc.dram_tensor("dbg_h", [128, 16384], F32,
                               kind="ExternalOutput")
        dbg_w1 = nc.dram_tensor("dbg_w1", [128, W1RES * 8 * 128], BF16,
                                kind="ExternalOutput")

    with ExitStack() as st:
        e = st.enter_context

        arena_a = e(nc.sbuf_tensor("arena_a", [128, 16384], F32))
        aa_bf = arena_a[:].bitcast(BF16)
        Wq16 = aa_bf[:, 0:8192]
        Wv16 = aa_bf[:, 8192:16384]
        Wo16h = aa_bf[:, 16384:32768]
        harena = arena_a
        ha_bf = aa_bf

        arena_b = e(nc.sbuf_tensor("arena_b", [128, 8192], F32))
        ab_bf = arena_b[:].bitcast(BF16)
        wbuf = [arena_b[:, 0:4096], arena_b[:, 4096:8192]]
        M16 = ab_bf[:, 0:8192]
        xqTw = ab_bf[:, 8192:12288]
        xqTw2 = ab_bf[:, 12288:16384]

        wqtmp = arena_a[:, 8192:12288]
        wqout = aa_bf[:, 24576:32768]
        w1res = e(nc.sbuf_tensor("w1res", [128, W1RES * 8 * 128], BF16))
        w1st = [e(nc.sbuf_tensor(f"w1st{i}", [128, 1024], BF16))
                for i in range(2)]
        w2st = [e(nc.sbuf_tensor(f"w2st{i}", [128, 1024], BF16))
                for i in range(2)]

        x_sb = [e(nc.sbuf_tensor(f"x_sb{i}", [128, 1024], F32))
                for i in range(2)]
        xln_sb = [e(nc.sbuf_tensor(f"xln_sb{i}", [128, 1024], BF16))
                  for i in range(2)]
        xlt_sb = [e(nc.sbuf_tensor(f"xlt_sb{i}", [128, 1024], BF16))
                  for i in range(2)]
        e_sb = [e(nc.sbuf_tensor(f"e_sb{i}", [128, 1024], F32))
                for i in range(2)]
        q16_sb = [e(nc.sbuf_tensor(f"q16_sb{i}", [128, 1024], BF16))
                  for i in range(2)]
        eq_sb = [e(nc.sbuf_tensor(f"eq_sb{i}", [128, 1024], BF16))
                 for i in range(2)]
        vaug_sb = [e(nc.sbuf_tensor(f"vaug_sb{i}", [128, 1088], BF16))
                   for i in range(2)]
        x1_sb = [e(nc.sbuf_tensor(f"x1_sb{i}", [128, 1024], F32))
                 for i in range(2)]
        xq_sb = [e(nc.sbuf_tensor(f"xq_sb{i}", [128, 1024], BF16))
                 for i in range(2)]
        x1r_sb = [e(nc.sbuf_tensor(f"x1r_sb{i}", [128, 1024], F32))
                  for i in range(2)]
        qt_sb = eq_sb
        ysc_sb = [xln_sb[0][:].bitcast(F32), xln_sb[1][:].bitcast(F32)]
        usc_sb = [xlt_sb[0][:].bitcast(F32), xlt_sb[1][:].bitcast(F32)]
        scrS = e(nc.sbuf_tensor("scrS", [128, 1024], F32))
        scrW = e_sb[0][:]
        scrR = e_sb[1][:]
        scr2 = e(nc.sbuf_tensor("scr2", [128, 1024], F32))
        amx_sb = q16_sb[0][:].bitcast(F32)[:, 0:512]
        srow_sb = e(nc.sbuf_tensor("srow", [1, 512], F32))
        srep_sb = q16_sb[1][:].bitcast(F32)[:, 0:512]
        ctx_sb = x_sb[0][0:65, :]
        ctx16_sb = x_sb[1][:].bitcast(BF16)[0:64, 0:1024]
        sfac_sb = e(nc.sbuf_tensor("sfac", [128, 8], F32))
        amn_sb = e(nc.sbuf_tensor("amn", [128, 512], F32))
        wo32_sb = [x1_sb[0][0:64, :], x1_sb[1][0:64, :]]

        ones16 = e(nc.sbuf_tensor("ones16", [1, 128], BF16))
        ones32c = e(nc.sbuf_tensor("ones32c", [128, 1], F32))
        ones32r = e(nc.sbuf_tensor("ones32r", [1, 128], F32))
        id32 = e(nc.sbuf_tensor("id32", [128, 128], F32))
        allones = e(nc.sbuf_tensor("allones", [128, 128], F32))
        bqv16 = e(nc.sbuf_tensor("bqv16", [1, 3072], BF16))
        cw_sb = e(nc.sbuf_tensor("cw_sb", [128, 128], F32))
        st1 = e(nc.sbuf_tensor("st1", [128, 32], F32))
        st2 = e(nc.sbuf_tensor("st2", [128, 32], F32))
        wst = e(nc.sbuf_tensor("wst", [128, 16], F32))
        srow1 = e_sb[1][0:1, :]
        srow2 = scr2[0:1, :]
        srow3 = e_sb[0][0:1, :]
        qh16 = e(nc.sbuf_tensor("qh16", [128, 8], BF16))
        par_sb = e(nc.sbuf_tensor("par_sb", [1, 2], F32))
        wsc_sb = e(nc.sbuf_tensor("wsc_sb", [128, 8], F32))
        cpool = e(nc.sbuf_tensor("cpool", [128, 4], F32))
        jnk = e(nc.sbuf_tensor("jnk", [1, 512], F32))
        _CONSTS = [EPS, MAGIC, -MAGIC, -0.5]
        for _ci, _cv in enumerate(_CONSTS):
            nc.const_aps.register(F32, _cv, cpool[:, _ci:_ci + 1])

        P_A = e(nc.psum_tensor("P_A", [128, 1024], F32))
        P_B = e(nc.psum_tensor("P_B", [128, 1024], F32))
        P_C = e(nc.psum_tensor("P_C", [128, 1024], F32))
        P_D = e(nc.psum_tensor("P_D", [128, 1024], F32))
        PQ = {0: P_C[:, 0:512], 1: P_C[:, 512:1024],
              2: P_D[:, 0:512], 3: P_D[:, 512:1024]}
        PT = [P_A, P_B, P_C, P_D]

        sV = e(nc.semaphore("sV"))
        sS = e(nc.semaphore("sS"))
        sP = e(nc.semaphore("sP"))
        sG = e(nc.semaphore("sG"))
        sCC = e(nc.semaphore("sCC"))
        sFN = e(nc.semaphore("sFN"))
        sFNS = e(nc.semaphore("sFNS"))
        sFNV = e(nc.semaphore("sFNV"))
        sFNG = e(nc.semaphore("sFNG"))
        sem = {"sV": sV, "sS": sS, "sP": sP, "sG": sG, "sFN": sFN,
               "sFNS": sFNS, "sFNV": sFNV, "sFNG": sFNG}
        cnt = {"sV": 0, "sS": 0, "sP": 0, "sG": 0, "sFN": 0,
               "sFNS": 0, "sFNV": 0, "sFNG": 0}
        _DSTREAMS = ["c0", "xa0", "xa1", "lnst", "xlt0", "xlt1", "qst",
                     "p2", "wo0", "wo1", "qt0", "qt1", "x30", "x31",
                     "x1s", "spill", "wxq", "xr0", "xr1", "yo",
                     "abi", "awl", "aml", "aql", "aqs",
                     "gbq", "gqr", "gcc", "gh", "gw1a", "gw1b", "aqlb",
                     "gw2a", "gw2b"]
        for _sn in _DSTREAMS:
            sem[_sn] = e(nc.semaphore("sem_" + _sn))
            cnt[_sn] = 0
        blk = e(nc.Block())

        def inc(instr, name, n=1):
            instr.then_inc(sem[name], n)
            cnt[name] += n

        def setev(name):
            def _set(evname, semname):
                tick[evname] = (semname, cnt[semname])
            return _set

        def ev(name, semname):
            tick[name] = (semname, cnt[semname])

        def mkdma(eng):
            def sdma(stream, out, in_, gate=False, **kw):
                if gate and cnt[stream] > 0:
                    eng.wait_ge(sem[stream], cnt[stream])
                it = eng.dma_start(out=out, in_=in_, **kw)
                it.then_inc(sem[stream], 16)
                cnt[stream] += 16
                return cnt[stream]
            return sdma

        def wts(eng, *names):
            best = {}
            for name in names:
                if name is None:
                    continue
                if name not in tick:
                    if not plan:
                        _MISSING.add(name)
                    continue
                s, v = tick[name]
                if v > 0:
                    best[s] = max(best.get(s, 0), v)
            for s, v in best.items():
                eng.wait_ge(sem[s], v)

        wt = wts

        def _sync_epilogue(sync):
            def dma(out, in_, **kw):
                it = sync.dma_start(out=out, in_=in_, **kw)
                it.then_inc(sem["yo"], 16)
                cnt["yo"] += 16

            if DBG:
                wt(sync, "vector_done")
                wt(sync, "scalar_done")
                wt(sync, "gp_done")
                wts(sync, *["__tot_" + s for s in _DSTREAMS if s != "yo"])
                if STAGE >= 1:
                    for t in range(NT1):
                        dma(dbg_xln[t * 128:(t + 1) * 128, :],
                            xln_hbm[t * 128:(t + 1) * 128, :])
                if STAGE >= 2:
                    dma(dbg_st[:, 0:8], wsc_sb[:])
                    dma(dbg_st[:, 8:14], wst[:, 2:8])
                    dma(dbg_st[:, 16:24], st2[:, 16:24])
                    dma(dbg_w1[:, :], w1res[:, 0:W1RES * 8 * 128])
                    for ch in range(32):
                        dma(dbg_wq2[ch], wq2_hbm[ch])
                if STAGE >= 3:
                    for t in range(NT1B):
                        dma(dbg_q[t * 128:(t + 1) * 128, :],
                            q_hbm[t * 128:(t + 1) * 128, :])
                    dma(dbg_cc[:, :],
                        cc_in[:, :] if STAGE < 4 else cc_out[:, :])
                if STAGE >= 4:
                    dma(dbg_M[:, :], M16)
                if STAGE >= 5:
                    for t in range(NT1):
                        dma(dbg_x1[t * 128:(t + 1) * 128, :],
                            x1_hbm[t * 128:(t + 1) * 128, :])
                    for t in range(XQW // 128):
                        dma(dbg_xqt[t * 128:(t + 1) * 128, :],
                            xq_hbm[t * 128:(t + 1) * 128, :])
                if STAGE >= 6:
                    dma(dbg_h[:, :], harena[:])
            wts(sync, *["__tot_" + s for s in _DSTREAMS if s != "yo"])
            if cnt["yo"] > 0:
                sync.wait_ge(sem["yo"], cnt["yo"])

        def _sync_body(sync):
            sdma = mkdma(sync)

            sdma("c0", par_sb[:], par_d[:, :])
            for i, src in enumerate([cw_d[0:1, :], cw_d[1:2, :],
                                     cw_d[2:3, :], cb_d[0:1, :]]):
                sdma("c0", cw_sb[:, i * 32:(i + 1) * 32],
                     src.rearrange("r (cc p) -> p (r cc)", p=128),
                     allow_slow_non_contiguous=True)
            ev("c_small", "c0")

            for t in range(NT1):
                if t >= 2:
                    wts(sync, f"1a_xln_{t-2}", f"1a_sq_{t-2}")
                xs = "xa0" if t % 2 == 0 else "xa1"
                sdma(xs, x_sb[t % 2][:], x_in[t * 128:(t + 1) * 128, :])
                ev(f"1a_x_{t}", xs)
                wt(sync, f"1a_xln_{t}")
                sdma("lnst", xln_hbm[t * 128:(t + 1) * 128, :],
                     xln_sb[t % 2][:], gate=True)
                ev(f"1a_st_{t}", "lnst")

            for t in range(NT1B if STAGE >= 3 else 0):
                if t >= 2:
                    wt(sync, f"1b_pe_v_{t-2}")
                wt(sync, f"1a_st_{t}")
                xls = "xlt0" if t % 2 == 0 else "xlt1"
                for g in range(8):
                    sdma(xls, xlt_sb[t % 2][:, g * 128:(g + 1) * 128],
                         xln_hbm[t * 128:(t + 1) * 128,
                                 g * 128:(g + 1) * 128],
                         transpose=True)
                ev(f"1b_xlt_{t}", xls)
                wt(sync, f"1b_q16_{t}")
                sdma("qst", q_hbm[t * 128:(t + 1) * 128, :], q16_sb[t % 2][:],
                     gate=True)
                ev(f"1b_qst_{t}", "qst")

            if STAGE < 4:
                raise StopEmission
            sync.wait_ge(sCC, 1)
            sdma("p2", ctx_sb, cc_out[0:65, :])
            sdma("p2", sfac_sb[:], cc_out[64:65, :].rearrange(
                "r (a two d) -> (two d) (r a)", a=8, two=2, d=64),
                allow_slow_non_contiguous=True)
            wt(sync, "v_bias")
            sdma("p2", srow1, cc_out[65:66, :])
            sdma("p2", srow2, cc_out[66:67, :])
            sdma("p2", srow3, x_halo[:, :])
            ev("p2_ld", "p2")
            for h in range(H):
                if h >= 2:
                    wt(sync, f"p2_wo_v_{h-2}")
                ws = "wo0" if h % 2 == 0 else "wo1"
                sdma(ws, wo32_sb[h % 2], Wo[h * 64:(h + 1) * 64, :])
                ev(f"p2_wo_{h}", ws)

            if STAGE < 5:
                raise StopEmission
            wt(sync, "g_pad")
            for phase_idx, tiles in enumerate(A_SCHED):
                for t in tiles:
                    if t >= 2:
                        wts(sync, f"a_pe_{t-2}")
                    else:
                        wts(sync, "p2_halo_pe", "p2_ctx16",
                            f"p2_wo_v_{H-1}")
                    qs = "qt0" if t % 2 == 0 else "qt1"
                    x3 = "x30" if t % 2 == 0 else "x31"
                    for g in range(8):
                        sdma(qs, qt_sb[t % 2][:, g * 128:(g + 1) * 128],
                             q_hbm[t * 128:(t + 1) * 128,
                                   g * 128:(g + 1) * 128],
                             transpose=True)
                    if t >= 2:
                        wt(sync, f"a_x1_{t-2}")
                    sdma(x3, x_sb[t % 2][:], x_in[t * 128:(t + 1) * 128, :])
                    ev(f"a_ld_{t}", x3)
                    ev(f"a_qt_{t}", qs)
                    wt(sync, f"a_x1_{t}")
                    sdma("x1s", x1_hbm[t * 128:(t + 1) * 128, :],
                         x1_sb[t % 2][:], gate=True)
                    ev(f"a_x1s_{t}", "x1s")
                    wt(sync, f"a_xq_{t}")
                    sdma("spill", xq_hbm[1 + 128 * t:1 + 128 * (t + 1), :],
                         xq_sb[t % 2][:], gate=True)
                    ev(f"a_spill_{t}", "spill")
                if phase_idx == 0:
                    continue
                g = phase_idx - 1
                if STAGE < 6 or g >= WLIM:
                    continue
                W0 = g * GW
                ntt = _ceil(GOUT[g], 128)
                if g >= 1:
                    wt(sync, f"w_w1done_{g-1}")
                wt(sync, f"a_spill_{min(_ceil(W0 + 512, 128), NT1) - 1}")
                for k in range(8):
                    dstw = xqTw if k < 4 else xqTw2
                    sdma("wxq", dstw[:, (k % 4) * 512:(k % 4 + 1) * 512],
                         xq_hbm[W0:W0 + 512, k * 128:(k + 1) * 128],
                         transpose=True)
                ev(f"w_xqT_{g}", "wxq")
                for ch in range(32):
                    w2s = "gw2a" if ch % 2 == 0 else "gw2b"
                    if g == 0:
                        wt(sync, f"wqst_1_{ch}")
                    if ch >= 2:
                        wt(sync, f"w_pe_w2_{g}_{ch-2}")
                    elif g > 0:
                        wt(sync, f"w_pe_w2_{g-1}_{30 + ch}")
                    sdma(w2s, w2st[ch % 2][:], wq2_hbm[ch])
                    ev(f"w2s_{g}_{ch}", w2s)

                def _ost(tt):
                    r0 = W0 + tt * 128
                    pr = min(128, W0 + GOUT[g] - r0)
                    wt(sync, f"w_out_{g}_{tt}")
                    sdma("yo", y_out[r0:r0 + pr, :], x1r_sb[tt % 2][0:pr, :])
                    ev(f"w_ost_{g}_{tt}", "yo")

                ntt_prev = _ceil(GOUT[g - 1], 128) if g >= 1 else 0
                for tt in range(ntt):
                    r0 = W0 + tt * 128
                    pr = min(128, W0 + GOUT[g] - r0)
                    if tt >= 2:
                        _ost(tt - 2)
                        wt(sync, f"w_ost_{g}_{tt-2}")
                    elif g >= 1:
                        wt(sync, f"w_ost_{g-1}_{ntt_prev-1}")
                    wt(sync, f"a_x1s_{min(_ceil(r0 + pr, 128), NT1) - 1}")
                    xr = "xr0" if tt % 2 == 0 else "xr1"
                    sdma(xr, x1r_sb[tt % 2][0:pr, :], x1_hbm[r0:r0 + pr, :])
                    ev(f"w_x1r_{g}_{tt}", xr)
                for tt in range(max(0, ntt - 2), ntt):
                    _ost(tt)
            raise StopEmission

        blk.sync(_stage_guard(_sync_body, _sync_epilogue))

        def _fin_scalar(eng):
            inc(eng.nop(), "sFNS")
            ev("scalar_done", "sFNS")

        def _fin_vector(eng):
            inc(eng.nop() if hasattr(eng, "nop") else eng._v.nop(), "sFNV")
            ev("vector_done", "sFNV")

        def _fin_gp(eng):
            inc(eng.nop(), "sFNG")
            ev("gp_done", "sFNG")

        def _fin_tensor(eng):
            inc(eng.nop(), "sFN")
            ev("tensor_done", "sFN")

        def _scalar_body(scalar):
            scalar = _ScalGuard(scalar)
            sdma = mkdma(scalar)

            def incd(_it, n=1):
                it2 = scalar.drain()
                it2.then_inc(sem["sS"], n)
                cnt["sS"] += n

            def sdelay(n=3):
                for _ in range(n):
                    scalar.activation(jnk[0:1, 256:384], jnk[0:1, 384:512],
                                      AF.Copy)

            def act(out, in_, func, **kw):
                return scalar.activation(out, in_, func, **kw)

            scalar.wait_ge(sV, 1)
            sdma("abi", srow1, bv_d[:, :])
            sdma("abi", srow2, bo_d[:, :])
            ev("bias_ld", "abi")

            wldix = [0]
            for wname, wt_d in enumerate([Wq, Wv]):
                for gch in range(8):
                    i = wldix[0]
                    if i >= 2:
                        pi = i - 2
                        wt(scalar, f"wcast_{pi // 8}_{pi % 8}")
                    sdma("awl", wbuf[gch % 2][:, 0:1024],
                         wt_d[gch * 128:(gch + 1) * 128, :], gate=True)
                    ev(f"wld_{wname}_{gch}", "awl")
                    wldix[0] += 1

            wt(scalar, "v_bias")
            for t in range(NT1):
                wts(scalar, f"1a_x_{t}",
                    f"1a_var_{t-1}" if t >= 1 else None)
                it = act(scrS[:], x_sb[t % 2][:], AF.Square,
                         accum_out=st1[:, 4:5])
                incd(it)
                ev(f"1a_sq_{t}", "sS")
                wts(scalar, f"1a_var_{t}",
                    f"1a_xln_{t-1}" if t >= 1 else None)
                act(st1[:, 8:9], st1[:, 3:4], AF.Ln, bias=EPS)
                it = act(st1[:, 6:7], st1[:, 8:9], AF.Exp, scale=-0.5)
                incd(it)
                ev(f"1a_rstd_{t}", "sS")

            if STAGE < 2:
                raise StopEmission
            mseq = [(0, c) for c in range(8)] + [(1, c) for c in range(32)]
            for i, (wi, ch) in enumerate(mseq):
                wt_d = w1T if wi == 0 else w2T
                width = DHID if wi == 0 else D
                if i == 0:
                    wt(scalar, "wcast_1_7")
                elif i == 1:
                    wt(scalar, "wcast_1_6")
                else:
                    pwi, pch = mseq[i - 2]
                    wt(scalar, f"wmean_{pwi}_{pch}")
                sdma("aml", wbuf[ch % 2][:, 0:width],
                     wt_d[ch * 128:(ch + 1) * 128, :], gate=True)
                ev(f"wmld_{wi}_{ch}", "aml")
            def _s_wq_rounds():
                qseq = [(0, c) for c in range(8)] + [(1, c) for c in range(32)]

                def _ld(i):
                    wi, ch = qseq[i]
                    wt_d = w1T if wi == 0 else w2T
                    width = DHID if wi == 0 else D
                    if i < 2:
                        wt(scalar, "wmean_1_31" if i == 0 else "wmean_1_30")
                        wt(scalar, "wsc_ready")
                    else:
                        pwi, pch = qseq[i - 2]
                        wt(scalar, f"wq_v_{pwi}_{pch}")
                    sdma("aql" if ch % 2 == 0 else "aqlb",
                         wbuf[ch % 2][:, 0:width],
                         wt_d[ch * 128:(ch + 1) * 128, :])
                    ev(f"wqld_{wi}_{ch}", "aql" if ch % 2 == 0 else "aqlb")

                _ld(0)
                for i, (wi, ch) in enumerate(qseq):
                    width = DHID if wi == 0 else D
                    if i + 1 < len(qseq):
                        _ld(i + 1)
                    wt(scalar, f"wqld_{wi}_{ch}")
                    it = act(wqtmp[:, 0:width], wbuf[ch % 2][:, 0:width],
                             AF.Identity, scale=wsc_sb[:, wi:wi + 1],
                             bias=MAGIC)
                    inc(it, "sS")
                    ev(f"wq_aff_{wi}_{ch}", "sS")
                    wt(scalar, f"wq_v_{wi}_{ch}")
                    if wi == 0:
                        gatefirst = True
                        for cc in range(W1RES, 32):
                            sdma("aqs", wq1_hbm[cc - W1RES, :, ch, :],
                                 wqout[:, cc * 128:(cc + 1) * 128],
                                 gate=gatefirst)
                            gatefirst = False
                    else:
                        sdma("aqs", wq2_hbm[ch, :, :], wqout[:, 0:1024],
                             gate=True)
                    ev(f"wqst_{wi}_{ch}", "aqs")

            if STAGE == 2:
                _s_wq_rounds()

            if STAGE < 3:
                raise StopEmission
            for t in range(NT1B):
                wt(scalar, f"1b_pe_q_{t}")
                if t >= 2:
                    wt(scalar, f"1b_q16_{t-2}")
                sdelay(1)
                PQB = P_B if t % 2 == 0 else P_D
                act(e_sb[t % 2][:, 0:512], PQB[:, 0:512], AF.Exp)
                it = act(e_sb[t % 2][:, 512:1024], PQB[:, 512:1024], AF.Exp)
                inc(it, "sS")
                ev(f"1b_s_exp_{t}", "sS")
                wt(scalar, f"1b_pe_vmm_{t}")
                sdelay(1)
                va = vaug_sb[t % 2][:].rearrange("p (h s) -> p h s", s=68)
                act(va[:, 0:8, 0:64],
                    P_C[:, 0:512].rearrange("p (h d) -> p h d", d=64),
                    AF.Copy)
                it = act(va[:, 8:16, 0:64],
                         P_C[:, 512:1024].rearrange("p (h d) -> p h d", d=64),
                         AF.Copy)
                inc(it, "sS")
                ev(f"1b_s_v_{t}", "sS")
                if t >= 1:
                    wt(scalar, f"1b_q16_{t-1}")
                    it = act(eq_sb[(t - 1) % 2][:], q16_sb[(t - 1) % 2][:],
                             AF.Exp)
                    inc(it, "sS")
                    ev(f"1b_s_eq_{t-1}", "sS")
            wt(scalar, f"1b_q16_{NT1B-1}")
            it = act(eq_sb[(NT1B - 1) % 2][:], q16_sb[(NT1B - 1) % 2][:],
                     AF.Exp)
            inc(it, "sS")
            ev(f"1b_s_eq_{NT1B-1}", "sS")
            _s_wq_rounds()

            if STAGE < 4:
                raise StopEmission
            wt(scalar, "p2_halo_pe")
            sdelay()
            act(scrS[0:1, :], P_B[0:1, :], AF.Square,
                accum_out=st1[0:1, 0:1])
            it = act(scrS[0:1, :], P_B[0:1, :], AF.Copy,
                     accum_out=st1[0:1, 2:3])
            incd(it)
            ev("p2_s_stats", "sS")
            wt(scalar, "p2_v_var")
            act(st1[0:1, 4:5], st1[0:1, 3:4], AF.Ln, bias=EPS)
            it = act(st1[0:1, 5:6], st1[0:1, 4:5], AF.Exp, scale=-0.5)
            incd(it)
            ev("p2_s_rstd", "sS")
            wt(scalar, "p2_v_x1h")
            act(scrS[0:1, :], srow3, AF.Square, accum_out=st1[0:1, 8:9])
            it = act(scrS[0:1, :], srow3, AF.Copy,
                     accum_out=st1[0:1, 10:11])
            incd(it)
            ev("p2_s_stats2", "sS")
            wt(scalar, "p2_v_var2")
            act(st1[0:1, 12:13], st1[0:1, 11:12], AF.Ln, bias=EPS)
            it = act(st1[0:1, 13:14], st1[0:1, 12:13], AF.Exp, scale=-0.5)
            incd(it)
            ev("p2_s_rstd2", "sS")

            if STAGE < 5:
                raise StopEmission
            prev_t = [None]
            for phase_idx, tiles in enumerate(A_SCHED):
                for t in tiles:
                    psA = P_A if t % 2 == 0 else P_B
                    wts(scalar, f"a_pe_{t}",
                        None if prev_t[0] is None else
                        f"a_v_bc_{prev_t[0]}")
                    prev_t[0] = t
                    sdelay(1)
                    act(scrS[:, 0:512], psA[:, 0:512], AF.Copy,
                        accum_out=st1[:, 0:1])
                    act(scrS[:, 512:1024], psA[:, 512:1024], AF.Copy,
                        accum_out=st1[:, 1:2])
                    act(scrS[:, 0:512], psA[:, 0:512], AF.Square,
                        accum_out=st1[:, 2:3])
                    it = act(scrS[:, 512:1024], psA[:, 512:1024], AF.Square,
                             accum_out=st1[:, 3:4])
                    incd(it)
                    ev(f"a_s_st_{t}", "sS")
                    wt(scalar, f"a_v_var_{t}")
                    act(st1[:, 8:9], st1[:, 6:7], AF.Ln, bias=EPS)
                    it = act(st1[:, 9:10], st1[:, 8:9], AF.Exp, scale=-0.5)
                    incd(it)
                    ev(f"a_s_rstd_{t}", "sS")
                    wt(scalar, f"a_x1_{t}")
                    it = act(scrS[:], x1_sb[t % 2][:], AF.Square,
                             accum_out=st2[:, 0:1])
                    incd(it)
                    ev(f"a_s_sq2_{t}", "sS")
                    wt(scalar, f"a_v_var2_{t}")
                    act(st2[:, 4:5], st2[:, 3:4], AF.Ln, bias=EPS)
                    it = act(st2[:, 5:6], st2[:, 4:5], AF.Exp, scale=-0.5)
                    incd(it)
                    ev(f"a_s_rstd2_{t}", "sS")
                    wt(scalar, f"a_v_nm2_{t}")
                    it = act(scr2[:], x1_sb[t % 2][:], AF.Identity,
                             scale=st2[:, 5:6], bias=st2[:, 6:7])
                    inc(it, "sS")
                    ev(f"a_s_xln2_{t}", "sS")
                    wt(scalar, f"a_v_sc_{t}")
                    it = act(scrR[:], scr2[:], AF.Identity,
                             scale=st2[:, 8:9], bias=MAGIC)
                    inc(it, "sS")
                    ev(f"a_s_rt_{t}", "sS")
                    wt(scalar, f"a_v_bc_{t}")
                    it = act(xq_sb[t % 2][:], scrR[:], AF.Identity,
                             scale=st2[:, 10:11], bias=st2[:, 11:12])
                    inc(it, "sS")
                    ev(f"a_xq_{t}", "sS")
                if phase_idx == 0:
                    continue
                g = phase_idx - 1
                if STAGE < 6 or g >= WLIM:
                    continue
                W0 = g * GW
                ntt = _ceil(GOUT[g], 128)
                for ch in range(32):
                    wts(scalar, f"w_pe_h_{g}_{ch}",
                        "c_small" if (g == 0 and ch == 0) else None)
                    it = act(harena[:, ch * 512:(ch + 1) * 512], PQ[ch % 4],
                             AF.Copy)
                    inc(it, "sS")
                    ev(f"w_hev_{g}_{ch}", "sS")
                    it = act(ysc_sb[ch % 2][:, 0:510],
                             harena[:, ch * 512:ch * 512 + 510],
                             AF.Identity, scale=cw_sb[:, ch:ch + 1],
                             bias=cw_sb[:, 96 + ch:97 + ch])
                    inc(it, "sS")
                    ev(f"w_cv1_{g}_{ch}", "sS")
                    wt(scalar, f"w_cv3_{g}_{ch}")
                    it = act(harena[:, ch * 512 + 1:ch * 512 + 511],
                             ysc_sb[ch % 2][:, 0:510], AF.Silu)
                    inc(it, "sS")
                    ev(f"w_silu_{g}_{ch}", "sS")
                    if ch >= 2:
                        wt(scalar, f"w_tr_{g}_{ch-2}")
                    it = act(usc_sb[ch % 2][:, 0:510],
                             harena[:, ch * 512 + 1:ch * 512 + 511], AF.Abs)
                    inc(it, "sS")
                    ev(f"w_abs_{g}_{ch}", "sS")
                def _s_rt(ch):
                    wt(scalar, f"w_u_{g}_{ch}")
                    it = act(ysc_sb[ch % 2][:, 0:510],
                             usc_sb[ch % 2][:, 0:510], AF.Identity,
                             bias=MAGIC)
                    inc(it, "sS")
                    ev(f"w_rt_{g}_{ch}", "sS")

                def _s_hq(ch):
                    wt(scalar, f"w_bc_{g}_{ch}")
                    it = act(ha_bf[:, ch * 1024 + 1:ch * 1024 + 511],
                             ysc_sb[ch % 2][:, 0:510], AF.Identity,
                             bias=-MAGIC)
                    inc(it, "sS")
                    ev(f"w_hq_{g}_{ch}", "sS")

                _s_rt(0)
                for ch in range(32):
                    if ch + 1 < 32:
                        _s_rt(ch + 1)
                    _s_hq(ch)
                for tt in range(ntt):
                    pr = min(128, GOUT[g] - tt * 128)
                    pst = PT[tt]
                    wts(scalar, f"w_pe_f_{g}",
                        f"w_v_fvar_{g}_{tt-1}" if tt >= 1 else None)
                    act(scrS[0:pr, 0:512], pst[0:pr, 0:512], AF.Copy,
                        accum_out=st1[0:pr, 16:17])
                    act(scrS[0:pr, 512:1024], pst[0:pr, 512:1024], AF.Copy,
                        accum_out=st1[0:pr, 17:18])
                    act(scrS[0:pr, 0:512], pst[0:pr, 0:512], AF.Square,
                        accum_out=st1[0:pr, 18:19])
                    it = act(scrS[0:pr, 512:1024], pst[0:pr, 512:1024],
                             AF.Square, accum_out=st1[0:pr, 19:20])
                    incd(it)
                    ev(f"w_s_fst_{g}_{tt}", "sS")
                    wt(scalar, f"w_v_fvar_{g}_{tt}")
                    act(st1[0:pr, 24:25], st1[0:pr, 23:24], AF.Ln, bias=EPS)
                    it = act(st1[0:pr, 25:26], st1[0:pr, 24:25], AF.Exp,
                             scale=-0.5)
                    incd(it)
                    ev(f"w_s_frstd_{g}_{tt}", "sS")


        blk.scalar(_stage_guard(_scalar_body, fin=_fin_scalar))

        def _vector_body(vector):
            vector = _VecGuard(vector)

            def incd(_it, n=1):
                it2 = vector.drain()
                it2.then_inc(sem["sV"], n)
                cnt["sV"] += n

            def vdelay(n=3):
                for _ in range(n):
                    vector.tensor_copy(jnk[0:1, 0:128], jnk[0:1, 128:256])


            def ts(out, in0, s1, s2, op0, op1=None, **kw):
                if op1 is None:
                    return vector.tensor_single_scalar(out, in0, s1, op0)
                return vector.tensor_scalar(out, in0, s1, s2, op0, op1, **kw)

            vector.memset(ones16[:], 1.0)
            vector.memset(ones32c[:], 1.0)
            for _ci, _cv in enumerate(_CONSTS):
                vector.memset(cpool[:, _ci:_ci + 1], _cv)
            vector.memset(ones32r[:], 1.0)
            vector.memset(allones[:], 1.0)
            vector.memset(jnk[:], 0.0)
            if DBG:
                vector.memset(wst[:], 0.0)
                vector.memset(st2[:], 0.0)
            for pbuf in range(2):
                va = vaug_sb[pbuf][:].rearrange("p (h s) -> p h s", s=68)
                vector.memset(va[:, :, 64:65], 1.0)
            it = vector.tensor_copy(st2[0:1, 31:32], ones32c[0:1, 0:1])
            inc(it, "sV")
            ev("v_const", "sV")

            wt(vector, "g_bq")
            vector.tensor_copy(bqv16[:, 0:1024], srow3)
            wt(vector, "bias_ld")
            vector.tensor_copy(bqv16[:, 1024:2048], srow1)
            it = vector.tensor_copy(bqv16[:, 2048:3072], srow2)
            inc(it, "sV")
            ev("v_bias", "sV")

            for wname in range(2):
                dst16 = Wq16 if wname == 0 else Wv16
                for gch in range(8):
                    wt(vector, f"wld_{wname}_{gch}")
                    it = vector.tensor_copy(
                        dst16[:, gch * 1024:(gch + 1) * 1024],
                        wbuf[gch % 2][:, 0:1024])
                    inc(it, "sV")
                    ev(f"wcast_{wname}_{gch}", "sV")

            for t in range(NT1):
                wt(vector, f"1a_x_{t}")
                vector.reduce_sum(st1[:, 0:1], x_sb[t % 2][:], axis=AX.X)
                ts(st1[:, 1:2], st1[:, 0:1], 1.0 / D, None, ALU.mult)
                vector.tensor_mul(st1[:, 2:3], st1[:, 1:2], st1[:, 1:2])
                wt(vector, f"1a_sq_{t}")
                it = ts(st1[:, 3:4], st1[:, 4:5], 1.0 / D, st1[:, 2:3],
                        ALU.mult, ALU.subtract)
                incd(it)
                ev(f"1a_var_{t}", "sV")
                wt(vector, f"1a_rstd_{t}")
                it = ts(xln_sb[t % 2][:], x_sb[t % 2][:], st1[:, 1:2],
                        st1[:, 6:7], ALU.subtract, ALU.mult)
                inc(it, "sV")
                ev(f"1a_xln_{t}", "sV")

            if STAGE < 2:
                raise StopEmission
            for wi, (nch, width) in enumerate([(8, DHID), (32, D)]):
                for ch in range(nch):
                    wt(vector, f"wmld_{wi}_{ch}")
                    vector.reduce_sum(wst[:, 0:1], wbuf[ch % 2][:, 0:width],
                                      axis=AX.X, apply_absolute_value=True)
                    if DBG and wi == 0:
                        vector.tensor_copy(wst[:, 4 + ch:5 + ch],
                                           wst[:, 0:1])
                    if ch == 0:
                        it = vector.tensor_copy(wst[:, 2 + wi:3 + wi],
                                                wst[:, 0:1])
                    else:
                        it = vector.tensor_add(wst[:, 2 + wi:3 + wi],
                                               wst[:, 2 + wi:3 + wi],
                                               wst[:, 0:1])
                    incd(it)
                    ev(f"wmean_{wi}_{ch}", "sV")
            ev("wmean_all", "sV")
            wt(vector, "w_srep_pe")
            vdelay()
            ts(wsc_sb[:, 6:8], P_D[:, 16:18], 1.0 / (D * DHID), None,
               ALU.mult)
            if DBG:
                vector.tensor_copy(st2[:, 22:24], wsc_sb[:, 6:8])
            ts(wsc_sb[:, 6:8], wsc_sb[:, 6:8], 1e-5, None, ALU.max)
            if DBG:
                vector.tensor_copy(st2[:, 18:20], wsc_sb[:, 6:8])
            vector.reciprocal(wsc_sb[:, 0:2], wsc_sb[:, 6:8])
            vector.reciprocal(wsc_sb[:, 2:4], wsc_sb[:, 0:2])
            ts(wsc_sb[:, 4:5], wsc_sb[:, 2:3], 1.0 / 127.0, None, ALU.mult)
            it = ts(wsc_sb[:, 5:6], wsc_sb[:, 3:4], 1.0 / 127.0, None,
                    ALU.mult)
            incd(it)
            ev("wsc_ready", "sV")

            def _v_wq_rounds():
                _wqprev = [None]
                for wi, (nch, width) in enumerate([(8, DHID), (32, D)]):
                    for ch in range(nch):
                        wts(vector, f"wq_aff_{wi}_{ch}",
                            None if _wqprev[0] is None else
                            "wqst_%d_%d" % _wqprev[0])
                        _wqprev[0] = (wi, ch)
                        vector.tensor_copy(
                            wbuf[ch % 2][:, 0:width].bitcast(I32),
                            wqtmp[:, 0:width].bitcast(I32))
                        ts(wqtmp[:, 0:width], wbuf[ch % 2][:, 0:width], MAGIC,
                           1.0, ALU.subtract, ALU.min)
                        it = ts(wqout[:, 0:width], wqtmp[:, 0:width], -1.0,
                                None, ALU.max)
                        inc(it, "sV")
                        if wi == 0:
                            for cc in range(W1RES):
                                it = vector.tensor_copy(
                                    w1res[:, (cc * 8 + ch) * 128:
                                          (cc * 8 + ch + 1) * 128],
                                    wqout[:, cc * 128:(cc + 1) * 128])
                            inc(it, "sV")
                        ev(f"wq_v_{wi}_{ch}", "sV")

            if STAGE == 2:
                _v_wq_rounds()
            if DBG and STAGE == 2:
                vector.tensor_copy(st2[:, 20:22], P_D[:, 16:18])
            if STAGE < 3:
                raise StopEmission
            for t in range(NT1B):
                wt(vector, f"1b_s_exp_{t}")
                e3 = e_sb[t % 2][:].rearrange("p (h d) -> p h d", d=64)
                vector.reduce_sum(st1[:, 0:16], e3, axis=AX.X)
                vector.reciprocal(st1[:, 16:32], st1[:, 0:16])
                ts(st2[:, 0:16], st1[:, 16:32], S_HEAD, None, ALU.mult)
                if t >= 2:
                    wt(vector, f"1b_qst_{t-2}")
                    wt(vector, f"1b_s_eq_{t-2}")
                for h in range(16):
                    it = ts(q16_sb[t % 2][:, h * 64:(h + 1) * 64],
                            e_sb[t % 2][:, h * 64:(h + 1) * 64],
                            st2[:, h:h + 1], None, ALU.mult)
                inc(it, "sV")
                ev(f"1b_q16_{t}", "sV")
            wt(vector, "1b_ctx_done")
            vdelay()
            vector.tensor_copy(ctx_sb[:, 0:512], P_A[0:65, 0:512])
            it = vector.tensor_copy(ctx_sb[:, 512:1024],
                                    P_A[0:65, 512:1024])
            inc(it, "sV")
            ev("v_ctx_ev", "sV")
            wts(vector, "g_qrows", "c_small")
            ts(srow1, srow1, par_sb[0:1, 0:1], None, ALU.mult)
            it = ts(srow2, srow2, par_sb[0:1, 1:2], None, ALU.mult)
            inc(it, "sV")
            ev("v_qstage", "sV")
            _v_wq_rounds()

            if STAGE < 4:
                raise StopEmission
            wt(vector, "p2_ld")
            vector.reciprocal(sfac_sb[:], sfac_sb[:])
            it = ts(sfac_sb[:], sfac_sb[:], S_HEAD, None, ALU.mult)
            incd(it)
            ev("p2_sfac", "sV")
            it = vector.tensor_copy(ctx16_sb, ctx_sb[0:64, :])
            inc(it, "sV")
            ev("p2_ctx16", "sV")
            wt(vector, "wqst_1_31")
            for h in range(H):
                wt(vector, f"p2_wo_{h}")
                it = vector.tensor_copy(
                    Wo16h[0:64, h * 1024:(h + 1) * 1024], wo32_sb[h % 2])
                inc(it, "sV")
                ev(f"p2_wo_v_{h}", "sV")
            for h in range(H):
                for jh in range(2):
                    wt(vector, f"p2_m_{h}_{jh}")
                    vdelay(2)
                    base = (h % 2) * 64
                    it = ts(M16[base:base + 64,
                                (h // 2) * 1024 + jh * 512:
                                (h // 2) * 1024 + jh * 512 + 512],
                            P_D[base:base + 64, jh * 512:jh * 512 + 512],
                            sfac_sb[base:base + 64, h // 2:h // 2 + 1],
                            None, ALU.mult)
                    inc(it, "sV")
                    ev(f"p2_mev_{h}_{jh}", "sV")
            ts(srow1, srow1, par_sb[0:1, 1:2], None, ALU.mult)
            it = vector.scalar_tensor_tensor(srow2, srow2,
                                             par_sb[0:1, 0:1], srow1,
                                             ALU.mult, ALU.add)
            inc(it, "sV")
            ev("p2_qh", "sV")
            wt(vector, "p2_s_stats")
            ts(st1[0:1, 16:17], st1[0:1, 2:3], 1.0 / D, None, ALU.mult)
            vector.tensor_mul(st1[0:1, 17:18], st1[0:1, 16:17],
                              st1[0:1, 16:17])
            it = ts(st1[0:1, 3:4], st1[0:1, 0:1], 1.0 / D, st1[0:1, 17:18],
                    ALU.mult, ALU.subtract)
            incd(it)
            ev("p2_v_var", "sV")
            wt(vector, "p2_s_rstd")
            vdelay(2)
            vector.tensor_mul(st1[0:1, 18:19], st1[0:1, 16:17],
                              st1[0:1, 5:6])
            ts(srow1, srow3, st1[0:1, 18:19], None, ALU.subtract)
            it = vector.scalar_tensor_tensor(srow3, P_B[0:1, :],
                                             st1[0:1, 5:6], srow1,
                                             ALU.mult, ALU.add)
            inc(it, "sV")
            ev("p2_v_x1h", "sV")
            wt(vector, "p2_s_stats2")
            ts(st1[0:1, 16:17], st1[0:1, 10:11], 1.0 / D, None, ALU.mult)
            vector.tensor_mul(st1[0:1, 17:18], st1[0:1, 16:17],
                              st1[0:1, 16:17])
            it = ts(st1[0:1, 11:12], st1[0:1, 8:9], 1.0 / D,
                    st1[0:1, 17:18], ALU.mult, ALU.subtract)
            incd(it)
            ev("p2_v_var2", "sV")
            wt(vector, "p2_s_rstd2")
            ts(srow1, srow3, st1[0:1, 16:17], st1[0:1, 13:14],
               ALU.subtract, ALU.mult)
            vector.reduce_max(st1[0:1, 20:21], srow1, axis=AX.X,
                              apply_absolute_value=True)
            ts(st1[0:1, 20:21], st1[0:1, 20:21], 1e-5, None, ALU.max)
            vector.reciprocal(st1[0:1, 21:22], st1[0:1, 20:21])
            ts(st1[0:1, 22:23], st1[0:1, 21:22], 127.0, None, ALU.mult)
            vector.tensor_mul(st1[0:1, 23:24], st1[0:1, 20:21],
                              wsc_sb[0:1, 4:5])
            ts(srow2, srow1, st1[0:1, 22:23], MAGIC, ALU.mult, ALU.add)
            vector.tensor_copy(srow1.bitcast(I32), srow2.bitcast(I32))
            it = ts(srow2, srow1, MAGIC, st1[0:1, 23:24],
                    ALU.subtract, ALU.mult)
            incd(it)
            ev("p2_v_xqh", "sV")
            ts(srow1, srow2, par_sb[0:1, 0:1], None, ALU.mult)
            it = ts(srow2, srow2, par_sb[0:1, 1:2], None, ALU.mult)
            inc(it, "sV")
            ev("p2_v_hmask", "sV")

            if STAGE < 5:
                raise StopEmission
            for phase_idx, tiles in enumerate(A_SCHED):
                for t in tiles:
                    psA = P_A if t % 2 == 0 else P_B
                    wt(vector, f"a_s_st_{t}")
                    vector.tensor_add(st1[:, 4:5], st1[:, 0:1], st1[:, 1:2])
                    vector.tensor_add(st1[:, 5:6], st1[:, 2:3], st1[:, 3:4])
                    ts(st1[:, 4:5], st1[:, 4:5], 1.0 / D, None, ALU.mult)
                    vector.tensor_mul(st1[:, 7:8], st1[:, 4:5], st1[:, 4:5])
                    it = ts(st1[:, 6:7], st1[:, 5:6], 1.0 / D, st1[:, 7:8],
                            ALU.mult, ALU.subtract)
                    incd(it)
                    ev(f"a_v_var_{t}", "sV")
                    wts(vector, f"a_s_rstd_{t}", f"a_ld_{t}")
                    vector.tensor_mul(st1[:, 10:11], st1[:, 4:5],
                                      st1[:, 9:10])
                    ts(scr2[:], x_sb[t % 2][:], st1[:, 10:11], None,
                       ALU.subtract)
                    it = vector.scalar_tensor_tensor(
                        x1_sb[t % 2][:], psA[:], st1[:, 9:10], scr2[:],
                        ALU.mult, ALU.add, accum_out=st2[:, 1:2])
                    inc(it, "sV")
                    ev(f"a_x1_{t}", "sV")
                    wt(vector, f"a_s_sq2_{t}")
                    ts(st2[:, 2:3], st2[:, 1:2], 1.0 / D, None, ALU.mult)
                    vector.tensor_mul(st2[:, 7:8], st2[:, 2:3], st2[:, 2:3])
                    it = ts(st2[:, 3:4], st2[:, 0:1], 1.0 / D, st2[:, 7:8],
                            ALU.mult, ALU.subtract)
                    incd(it)
                    ev(f"a_v_var2_{t}", "sV")
                    wt(vector, f"a_s_rstd2_{t}")
                    vector.tensor_mul(st2[:, 6:7], st2[:, 2:3], st2[:, 5:6])
                    it = ts(st2[:, 6:7], st2[:, 6:7], -1.0, None, ALU.mult)
                    incd(it)
                    ev(f"a_v_nm2_{t}", "sV")
                    wt(vector, f"a_s_xln2_{t}")
                    vector.reduce_max(st2[:, 9:10], scr2[:], axis=AX.X,
                                      apply_absolute_value=True)
                    ts(st2[:, 9:10], st2[:, 9:10], 1e-5, None, ALU.max)
                    vector.reciprocal(st2[:, 8:9], st2[:, 9:10])
                    ts(st2[:, 8:9], st2[:, 8:9], 127.0, None, ALU.mult)
                    vector.tensor_mul(st2[:, 10:11], st2[:, 9:10],
                                      wsc_sb[:, 4:5])
                    it = ts(st2[:, 11:12], st2[:, 10:11], -MAGIC, None,
                            ALU.mult)
                    incd(it)
                    ev(f"a_v_sc_{t}", "sV")
                    wt(vector, f"a_s_rt_{t}")
                    it = vector.tensor_copy(scrR[:].bitcast(I32),
                                            scrR[:].bitcast(I32))
                    inc(it, "sV")
                    ev(f"a_v_bc_{t}", "sV")
                if phase_idx == 0:
                    continue
                g = phase_idx - 1
                if STAGE < 6 or g >= WLIM:
                    continue
                W0 = g * GW
                ntt = _ceil(GOUT[g], 128)
                def _v_tree(ch):
                    wt(vector, f"w_abs_{g}_{ch}")
                    hw = usc_sb[ch % 2][:, 0:510]
                    if ch == 0:
                        it = vector.tensor_copy(amx_sb[:, 0:510], hw)
                    else:
                        it = vector.tensor_max(amx_sb[:, 0:510],
                                               amx_sb[:, 0:510], hw)
                    inc(it, "sV")
                    ev(f"w_tr_{g}_{ch}", "sV")

                for ch in range(32):
                    wt(vector, f"w_cv1_{g}_{ch}")
                    vector.scalar_tensor_tensor(
                        ysc_sb[ch % 2][:, 0:510],
                        harena[:, ch * 512 + 1:ch * 512 + 511],
                        cw_sb[:, 32 + ch:33 + ch], ysc_sb[ch % 2][:, 0:510],
                        ALU.mult, ALU.add)
                    it = vector.scalar_tensor_tensor(
                        ysc_sb[ch % 2][:, 0:510],
                        harena[:, ch * 512 + 2:ch * 512 + 512],
                        cw_sb[:, 64 + ch:65 + ch], ysc_sb[ch % 2][:, 0:510],
                        ALU.mult, ALU.add)
                    inc(it, "sV")
                    ev(f"w_cv3_{g}_{ch}", "sV")
                    if ch >= 1:
                        _v_tree(ch - 1)
                _v_tree(31)
                ev(f"w_tree_{g}", "sV")
                wt(vector, f"w_tp_{g}_3")
                vdelay()
                for tt in range(4):
                    vector.reduce_max(st1[:, 28 + tt:29 + tt],
                                      P_A[:, tt * 128:(tt + 1) * 128],
                                      axis=AX.X)
                ts(st1[:, 28:32], st1[:, 28:32], 1e-5, None, ALU.max)
                vector.reciprocal(st2[:, 24:28], st1[:, 28:32])
                ts(st2[:, 24:28], st2[:, 24:28], 127.0, None, ALU.mult)
                for tt in range(4):
                    it = vector.tensor_mul(st2[:, 28 + tt:29 + tt],
                                           st1[:, 28 + tt:29 + tt],
                                           wsc_sb[:, 5:6])
                incd(it)
                ev(f"w_sc_{g}", "sV")
                wt(vector, f"w_srow_pe_{g}")
                vdelay(2)
                it = vector.tensor_copy(srow_sb[:], P_B[0:1, 0:512])
                inc(it, "sV")
                ev(f"w_srow_{g}", "sV")
                wt(vector, f"w_srep_pe_{g}")
                vdelay(2)
                it = vector.tensor_copy(srep_sb, P_B[:, 512:1024])
                inc(it, "sV")
                ev(f"w_srep_{g}", "sV")
                for tt in range(ntt):
                    pr = min(128, GOUT[g] - tt * 128)
                    pst = PT[tt]
                    wt(vector, f"w_s_fst_{g}_{tt}")
                    vector.tensor_add(st1[0:pr, 20:21], st1[0:pr, 16:17],
                                      st1[0:pr, 17:18])
                    vector.tensor_add(st1[0:pr, 21:22], st1[0:pr, 18:19],
                                      st1[0:pr, 19:20])
                    ts(st1[0:pr, 20:21], st1[0:pr, 20:21], 1.0 / D, None,
                       ALU.mult)
                    vector.tensor_mul(st1[0:pr, 22:23], st1[0:pr, 20:21],
                                      st1[0:pr, 20:21])
                    ts(st1[0:pr, 21:22], st1[0:pr, 21:22], 1.0 / D,
                       st1[0:pr, 22:23], ALU.mult, ALU.subtract)
                    vector.tensor_mul(st1[0:pr, 22:23],
                                      st2[0:pr, 28 + tt:29 + tt],
                                      st2[0:pr, 28 + tt:29 + tt])
                    it = vector.tensor_mul(st1[0:pr, 23:24],
                                           st1[0:pr, 21:22],
                                           st1[0:pr, 22:23])
                    incd(it)
                    ev(f"w_v_fvar_{g}_{tt}", "sV")
                    wt(vector, f"w_s_frstd_{g}_{tt}")
                    vector.tensor_mul(st1[0:pr, 26:27],
                                      st2[0:pr, 28 + tt:29 + tt],
                                      st1[0:pr, 25:26])
                    vector.tensor_mul(st1[0:pr, 27:28], st1[0:pr, 20:21],
                                      st1[0:pr, 26:27])
                    wt(vector, f"w_x1r_{g}_{tt}")
                    ts(scrW[0:pr, :], x1r_sb[tt % 2][0:pr, :],
                       st1[0:pr, 27:28], None, ALU.subtract)
                    it = vector.scalar_tensor_tensor(
                        x1r_sb[tt % 2][0:pr, :], pst[0:pr, :],
                        st1[0:pr, 26:27], scrW[0:pr, :], ALU.mult, ALU.add)
                    inc(it, "sV")
                    ev(f"w_out_{g}_{tt}", "sV")
            ev("vector_done", "sV")

        blk.vector(_stage_guard(_vector_body, fin=_fin_vector))

        def _gp_body(gp):
            sdma = mkdma(gp)

            sdma("gbq", srow3, bq_d[:, :])
            ev("g_bq", "gbq")
            sdma("gbq", id32[:], ident_d[:, :], gate=True)
            ev("g_id", "gbq")


            if STAGE < 3:
                raise StopEmission
            wts(gp, f"1b_qst_{NT1B-1}", f"1b_q16_{NT1B-1}")
            sdma("gqr", srow1, q_hbm[0:1, :])
            _lq = NT1B * 128 - 1
            sdma("gqr", srow2, q_hbm[_lq:_lq + 1, :])
            ev("g_qrows", "gqr")
            wt(gp, "v_ctx_ev")
            sdma("gcc", cc_in[0:65, :], ctx_sb)
            wt(gp, "v_qstage")
            gp.wait_ge(sem["gqr"], cnt["gqr"])
            sdma("gcc", cc_in[65:66, :], srow1)
            sdma("gcc", cc_in[66:67, :], srow2)
            if STAGE < 4:
                raise StopEmission
            gp.wait_ge(sem["gcc"], cnt["gcc"])
            it = gp.collective_compute(
                "AllReduce", ALU.add,
                replica_groups=[[0, 1], [2, 3], [4, 5], [6, 7]],
                ins=[cc_in.ap().opt()],
                outs=[cc_out.ap().opt()],
            )
            it.then_inc(sCC, 1)
            gp.wait_ge(sCC, 1)

            wt(gp, "p2_qh")
            sdma("gh", hrow_hbm[:, :], srow2)
            gp.wait_ge(sem["gh"], cnt["gh"])
            sdma("gh", qh16[:, 0:8],
                 hrow_hbm[0:1, :].rearrange("r (g p) -> p (r g)", p=128),
                 allow_slow_non_contiguous=True)
            ev("p2_qhT", "gh")
            wt(gp, "p2_v_hmask")
            gp.wait_ge(sem["gh"], cnt["gh"])
            sdma("gh", xq_hbm[0:1, :], srow1)
            sdma("gh", xq_hbm[TOK + 1:TOK + 2, :], srow2)
            it = gp.memset(scrW[:, 0:1024], 0.0)
            inc(it, "sG")
            ev("g_pad0", "sG")
            wt(gp, "g_pad0")
            r = TOK + 2
            while r < XQW:
                nr = min(128, XQW - r)
                sdma("gh", xq_hbm[r:r + nr, :], scrW[0:nr, 0:1024])
                r += nr
            ev("g_pad", "gh")

            if STAGE < 6:
                raise StopEmission
            for g in range(min(NG, WLIM)):
                for ch in range(32):
                    if ch >= W1RES:
                        w1s = "gw1a" if ch % 2 == 0 else "gw1b"
                        if g == 0:
                            wt(gp, "wqst_0_7")
                        if ch >= W1RES + 2:
                            wt(gp, f"w_pe_h_{g}_{ch-2}")
                        elif g > 0:
                            wt(gp, f"w_pe_h_{g-1}_31")
                        sdma(w1s, w1st[ch % 2][:],
                             wq1_hbm[ch - W1RES].rearrange("p k c -> p (k c)"))
                        ev(f"w1s_{g}_{ch}", w1s)
                def _g_u(ch):
                    wt(gp, f"w_silu_{g}_{ch}")
                    if ch >= 2:
                        wt(gp, f"w_rt_{g}_{ch-2}")
                    it = gp.tensor_tensor(
                        usc_sb[ch % 2][:, 0:510],
                        harena[:, ch * 512 + 1:ch * 512 + 511],
                        srep_sb[:, 0:510], ALU.mult)
                    inc(it, "sG")
                    ev(f"w_u_{g}_{ch}", "sG")

                def _g_bc(ch):
                    wt(gp, f"w_rt_{g}_{ch}")
                    it = gp.tensor_copy(
                        ysc_sb[ch % 2][:, 0:510].bitcast(I32),
                        ysc_sb[ch % 2][:, 0:510].bitcast(I32))
                    inc(it, "sG")
                    ev(f"w_bc_{g}_{ch}", "sG")

                wt(gp, f"w_srep_{g}")
                _g_u(0)
                for ch in range(32):
                    if ch + 1 < 32:
                        _g_u(ch + 1)
                    _g_bc(ch)

        blk.gpsimd(_stage_guard(_gp_body, fin=_fin_gp))

        def _tensor_body(tensor):
            def mm(out, lhsT, rhs, **kw):
                return tensor.matmul(out, lhsT, rhs, **kw)

            if STAGE < 2:
                raise StopEmission
            wt(tensor, "v_const")
            wt(tensor, "wmean_all")
            it = mm(P_D[:, 16:18], allones[:], wst[:, 2:4],
                    start=True, stop=True)
            inc(it, "sP")
            ev("w_srep_pe", "sP")

            if STAGE < 3:
                raise StopEmission
            wt(tensor, "v_bias")
            for t in range(NT1B):
                wt(tensor, f"1b_xlt_{t}")
                if t >= 2:
                    wt(tensor, f"1b_s_exp_{t-2}")
                if t == 1:
                    wt(tensor, "wsc_ready")
                PQB = P_B if t % 2 == 0 else P_D
                for jh in range(2):
                    for gch in range(8):
                        mm(PQB[:, jh * 512:(jh + 1) * 512],
                           xlt_sb[t % 2][:, gch * 128:(gch + 1) * 128],
                           Wq16[:, gch * 1024 + jh * 512:
                                gch * 1024 + jh * 512 + 512],
                           start=(gch == 0), stop=False)
                    it = mm(PQB[:, jh * 512:(jh + 1) * 512], ones16[0:1, :],
                            bqv16[0:1, jh * 512:jh * 512 + 512],
                            start=False, stop=True)
                inc(it, "sP")
                ev(f"1b_pe_q_{t}", "sP")
                if t >= 1:
                    wt(tensor, f"1b_s_v_{t-1}")
                for jh in range(2):
                    for gch in range(8):
                        mm(P_C[:, jh * 512:(jh + 1) * 512],
                           xlt_sb[t % 2][:, gch * 128:(gch + 1) * 128],
                           Wv16[:, gch * 1024 + jh * 512:
                                gch * 1024 + jh * 512 + 512],
                           start=(gch == 0), stop=False)
                    it = mm(P_C[:, jh * 512:(jh + 1) * 512], ones16[0:1, :],
                            bqv16[0:1, 1024 + jh * 512:1024 + jh * 512 + 512],
                            start=False, stop=True)
                inc(it, "sP")
                ev(f"1b_pe_vmm_{t}", "sP")
                if t >= 1:
                    wt(tensor, f"1b_s_eq_{t-1}")
                    va = vaug_sb[(t - 1) % 2]
                    for h in range(16):
                        it = mm(P_A[0:65, h * 64:(h + 1) * 64],
                                va[:, h * 68:h * 68 + 65],
                                eq_sb[(t - 1) % 2][:, h * 64:(h + 1) * 64],
                                start=(t - 1 == 0 and h % 8 == 0),
                                stop=False)
                    inc(it, "sP")
                ev(f"1b_pe_v_{t}", "sP")
            wt(tensor, f"1b_s_eq_{NT1B-1}")
            va = vaug_sb[(NT1B - 1) % 2]
            for h in range(16):
                it = mm(P_A[0:65, h * 64:(h + 1) * 64],
                        va[:, h * 68:h * 68 + 65],
                        eq_sb[(NT1B - 1) % 2][:, h * 64:(h + 1) * 64],
                        start=(NT1B == 1 and h % 8 == 0),
                        stop=(h % 8 == 7))
            inc(it, "sP")
            ev("1b_ctx_done", "sP")

            if STAGE < 4:
                raise StopEmission
            wt(tensor, "p2_ctx16")
            wt(tensor, "p2_sfac")
            for h in range(H):
                wt(tensor, f"p2_wo_v_{h}")
                base = (h % 2) * 64
                for jh in range(2):
                    kw = {}
                    if base:
                        kw["tile_position"] = (0, 64)
                    it = mm(P_D[base:base + 64, jh * 512:jh * 512 + 512],
                            ctx16_sb[:, h * 64:(h + 1) * 64],
                            Wo16h[0:64, h * 1024 + jh * 512:
                                  h * 1024 + jh * 512 + 512],
                            start=True, stop=True, **kw)
                    inc(it, "sP")
                    ev(f"p2_m_{h}_{jh}", "sP")
                    wt(tensor, f"p2_mev_{h}_{jh}")
            wt(tensor, "p2_qhT")
            for jh in range(2):
                for g8 in range(8):
                    mm(P_B[0:1, jh * 512:jh * 512 + 512],
                       qh16[:, g8:g8 + 1],
                       M16[:, g8 * 1024 + jh * 512:g8 * 1024 + jh * 512 + 512],
                       start=(g8 == 0), stop=False)
                it = mm(P_B[0:1, jh * 512:jh * 512 + 512], ones16[0:1, 0:1],
                        bqv16[0:1, 2048 + jh * 512:2048 + jh * 512 + 512],
                        start=False, stop=True)
            inc(it, "sP")
            ev("p2_halo_pe", "sP")

            if STAGE < 5:
                raise StopEmission
            for phase_idx, tiles in enumerate(A_SCHED):
                for t in tiles:
                    psA = P_A if t % 2 == 0 else P_B
                    names = [f"a_ld_{t}", f"a_qt_{t}"]
                    if t >= 2:
                        names += [f"a_s_st_{t-2}", f"a_x1_{t-2}"]
                    if phase_idx == 0 and t == 0:
                        names.append("v_ctx_ev")
                    if t % 2 == 1 and phase_idx == 0 and t == 1:
                        names += ["p2_s_stats2", "p2_v_x1h"]
                    if phase_idx >= 2:
                        gprev = phase_idx - 2
                        names.append(
                            f"w_out_{gprev}_{_ceil(GOUT[gprev], 128) - 1}")
                    wts(tensor, *names)
                    for jh in range(2):
                        for g8 in range(8):
                            mm(psA[:, jh * 512:(jh + 1) * 512],
                               qt_sb[t % 2][:, g8 * 128:(g8 + 1) * 128],
                               M16[:, g8 * 1024 + jh * 512:
                                   g8 * 1024 + jh * 512 + 512],
                               start=(g8 == 0), stop=False)
                        it = mm(psA[:, jh * 512:(jh + 1) * 512],
                                ones16[0:1, :],
                                bqv16[0:1, 2048 + jh * 512:
                                      2048 + jh * 512 + 512],
                                start=False, stop=True)
                    inc(it, "sP")
                    ev(f"a_pe_{t}", "sP")
                if phase_idx == 0:
                    continue
                g = phase_idx - 1
                if STAGE < 6 or g >= WLIM:
                    continue
                W0 = g * GW
                ntt = _ceil(GOUT[g], 128)
                wt(tensor, f"w_xqT_{g}")
                for ch in range(32):
                    qtr = PQ[ch % 4]
                    names = []
                    if ch >= 4:
                        names.append(f"w_hev_{g}_{ch-4}")
                    elif g > 0:
                        names.append(f"w_hev_{g-1}_{28 + ch}")
                        if ch >= 2 and _ceil(GOUT[g - 1], 128) > ch:
                            names.append(f"w_out_{g-1}_{ch}")
                    if ch < W1RES:
                        lsrc = [w1res[:, (ch * 8 + k) * 128:
                                      (ch * 8 + k + 1) * 128]
                                for k in range(8)]
                        names.append("wq_v_0_7")
                    else:
                        names.append(f"w1s_{g}_{ch}")
                        lsrc = [w1st[ch % 2][:, k * 128:(k + 1) * 128]
                                for k in range(8)]
                    wts(tensor, *names)
                    for k in range(8):
                        rsrc = (xqTw if k < 4 else xqTw2)
                        it = mm(qtr, lsrc[k],
                                rsrc[:, (k % 4) * 512:(k % 4 + 1) * 512],
                                start=(k == 0), stop=(k == 7))
                    inc(it, "sP")
                    ev(f"w_pe_h_{g}_{ch}", "sP")
                ev(f"w_w1done_{g}", "sP")
                wt(tensor, f"w_tree_{g}")
                wt(tensor, "g_id")
                wt(tensor, f"a_x1_{tiles[-1]}" if tiles else None)
                for tt in range(4):
                    it = tensor.transpose(P_A[:, tt * 128:(tt + 1) * 128],
                                          amx_sb[:, tt * 128:(tt + 1) * 128],
                                          id32[:])
                    inc(it, "sP")
                    ev(f"w_tp_{g}_{tt}", "sP")
                wt(tensor, f"w_sc_{g}")
                for tt in range(4):
                    it = mm(P_B[0:1, tt * 128:(tt + 1) * 128],
                            st2[:, 24 + tt:25 + tt], id32[:],
                            start=(tt == 0), stop=(tt == 3))
                inc(it, "sP")
                ev(f"w_srow_pe_{g}", "sP")
                wt(tensor, f"w_srow_{g}")
                it = mm(P_B[:, 512:1024], ones32r[0:1, :], srow_sb[:],
                        start=True, stop=True)
                inc(it, "sP")
                ev(f"w_srep_pe_{g}", "sP")
                for ch in range(32):
                    names = [f"w2s_{g}_{ch}", f"w_hq_{g}_{ch}"]
                    if ch == 0:
                        if g:
                            names.append(
                                f"w_s_fst_{g-1}_{_ceil(GOUT[g-1], 128) - 1}")
                        names.append(f"w_srep_{g}")
                    wts(tensor, *names)
                    for tt in range(ntt):
                        pr = min(128, GOUT[g] - tt * 128)
                        pst = PT[tt]
                        lh = ha_bf[:, ch * 1024 + 1 + tt * 128:
                                   ch * 1024 + 1 + tt * 128 + pr]
                        for jh in range(2):
                            it = mm(pst[0:pr, jh * 512:(jh + 1) * 512],
                                    lh, w2st[ch % 2][:, jh * 512:
                                                     (jh + 1) * 512],
                                    start=(ch == 0), stop=(ch == 31))
                    inc(it, "sP")
                    ev(f"w_pe_w2_{g}_{ch}", "sP")
                ev(f"w_pe_f_{g}", "sP")

        blk.tensor(_stage_guard(_tensor_body, fin=_fin_tensor))

        if plan:
            for _sn in _DSTREAMS:
                tick["__tot_" + _sn] = (_sn, cnt[_sn])

    return nc


def _build():
    tick = {}
    build_nc(tick, plan=True)
    nc = build_nc(dict(tick), plan=False)
    return nc


def _host_in_maps(inputs):
    x = np.ascontiguousarray(np.asarray(inputs["L0_attn_x"],
                                        dtype=np.float32)).reshape(B * N, D)
    WqA = np.ascontiguousarray(inputs["L0_attn_Wq"], dtype=np.float32)
    WvA = np.ascontiguousarray(inputs["L0_attn_Wv"], dtype=np.float32)
    WoA = np.ascontiguousarray(inputs["L0_attn_Wo"], dtype=np.float32)
    bqA = np.asarray(inputs["L0_attn_bq"], dtype=np.float32).reshape(1, D)
    bvA = np.asarray(inputs["L0_attn_bv"], dtype=np.float32).reshape(1, D)
    boA = np.asarray(inputs["L0_attn_bo"], dtype=np.float32).reshape(1, D)
    w1TA = np.ascontiguousarray(
        np.asarray(inputs["L0_ffn_w1"], dtype=np.float32).T)
    w2TA = np.ascontiguousarray(
        np.asarray(inputs["L0_ffn_w2"], dtype=np.float32).T)
    cwA = np.ascontiguousarray(
        np.asarray(inputs["L0_ffn_conv_w"], dtype=np.float32)[:, 0, :]
        .reshape(DHID, 3).T)
    cbA = np.asarray(inputs["L0_ffn_conv_b"], dtype=np.float32).reshape(
        1, DHID)

    in_maps = []
    for c in range(NCORE):
        lo = c * TOK
        odd = c % 2
        halo_row = lo + TOK if odd == 0 else lo - 1
        halo = (x[halo_row:halo_row + 1]
                if 0 <= halo_row < B * N else np.zeros((1, D), np.float32))
        in_maps.append({
            "x_in": x[lo:lo + TOK],
            "x_halo": np.ascontiguousarray(halo),
            "Wq": WqA, "Wv": WvA, "Wo": WoA,
            "bq": bqA, "bv": bvA, "bo": boA,
            "w1T": w1TA, "w2T": w2TA, "cw": cwA, "cb": cbA,
            "par": np.array([[1.0 if odd else 0.0,
                              0.0 if odd else 1.0]], np.float32),
        })
    return in_maps


def run_raw(inputs):
    nc = _build()
    in_maps = _host_in_maps(inputs)
    return run_bass_kernel_spmd(nc, in_maps, core_ids=list(range(NCORE)))


def kernel(**inputs):
    res = run_raw(inputs)
    out = np.concatenate([np.asarray(r["y_out"]) for r in res.results],
                         axis=0)
    return out.reshape(B, N, D).astype(np.float32)


# version 13 (final)
# speedup vs baseline: 1.3637x; 1.0068x over previous
import numpy as np
import ml_dtypes
from contextlib import ExitStack
import concourse.bass as bass
import concourse.mybir as mybir
from concourse.bass_utils import run_bass_kernel_spmd

F32 = mybir.dt.float32
BF16 = mybir.dt.bfloat16
I32 = mybir.dt.int32
AF = mybir.ActivationFunctionType
ALU = mybir.AluOpType
AX = mybir.AxisListType

NCORE = 8
B, N, D = 4, 8192, 1024
H = 16
DHID = 4096
TOK = (B * N) // NCORE
NT1 = TOK // 128
S_HEAD = float((64 ** 0.5) ** (-0.25))
EPS = 1e-5
MAGIC = 12582912.0
GW = 510
NG = -(-TOK // GW)
GOUT = [min(GW, TOK - g * GW) for g in range(NG)]
XQW = 4608
W1RES = 12

DBG = False
STAGE = 7
WLIM = NG if True else 0
TLIM = 10**9


def _ceil(a, b):
    return -(-a // b)


def _a_sched():
    out = [list(range(0, 6))]
    nxt = 6
    for g in range(NG):
        need = min(NT1, _ceil(GW * (g + 1) + 2, 128) + 2)
        hi = min(max(nxt, need), NT1)
        if g == NG - 1:
            hi = NT1
        out.append(list(range(nxt, hi)))
        nxt = hi
    return out


A_SCHED = _a_sched()




_DT_SZ = {"dt.float32": 4, "dt.int32": 4, "dt.uint32": 4,
          "dt.bfloat16": 2, "dt.float16": 2, "dt.uint16": 2,
          "dt.int16": 2, "dt.uint8": 1, "dt.int8": 1}


def _apint(ap):
    if ap is None:
        return None
    try:
        sz = _DT_SZ.get(str(ap.dtype), 4)
        off = int(ap.offset)
        ext = 1
        for s, n in ap.ap:
            ext += int(s) * (int(n) - 1)
        return (ap.name, off * sz, (off + ext) * sz)
    except Exception:
        try:
            return (ap.name, 0, 1 << 62)
        except Exception:
            return None


class _WriteLog:

    def __init__(self):
        self._w = {}

    def clear(self):
        self._w.clear()

    def add(self, iv):
        if iv is not None:
            self._w.setdefault(iv[0], []).append((iv[1], iv[2]))

    def hits(self, iv):
        if iv is None:
            return False
        for lo, hi in self._w.get(iv[0], ()):
            if iv[1] < hi and lo < iv[2]:
                return True
        return False


class _HazardLog:

    def __init__(self):
        self._w = _WriteLog()
        self._r = _WriteLog()

    def clear(self):
        self._w.clear()
        self._r.clear()

    def hazard(self, rds, wrs):
        return (any(self._w.hits(iv) for iv in rds)
                or any(self._w.hits(iv) for iv in wrs)
                or any(self._r.hits(iv) for iv in wrs))

    def record(self, rds, wrs):
        for iv in rds:
            self._r.add(iv)
        for iv in wrs:
            self._w.add(iv)


class _ScalGuard:

    def __init__(self, sc):
        self._s = sc
        self._recent = _HazardLog()

    def wait_ge(self, s, v):
        return self._s.wait_ge(s, v)

    def drain(self, *a, **k):
        self._recent.clear()
        return self._s.drain(*a, **k)

    def nop(self):
        return self._s.nop()

    def dma_start(self, **kw):
        return self._s.dma_start(**kw)

    def activation(self, out, in_, func, bias=0.0, scale=1.0, **kw):
        rds = [_apint(in_)]
        if isinstance(bias, bass.AP):
            rds.append(_apint(bias))
        if isinstance(scale, bass.AP):
            rds.append(_apint(scale))
        wrs = [_apint(out)]
        ao = kw.get("accum_out")
        if ao is not None:
            wrs.append(_apint(ao))
        if self._recent.hazard(rds, wrs):
            self._s.drain()
            self._recent.clear()
        it = self._s.activation(out, in_, func, bias=bias, scale=scale, **kw)
        self._recent.record(rds, wrs)
        return it


class _VecGuard:

    def __init__(self, vec):
        self._v = vec
        self._recent = _HazardLog()

    def _emit(self, fn, outs, ins, *args, **kwargs):
        rds = [_apint(a) for a in ins if a is not None]
        wrs = [_apint(a) for a in outs if a is not None]
        if self._recent.hazard(rds, wrs):
            self._v.drain()
            self._recent.clear()
        it = fn(*args, **kwargs)
        self._recent.record(rds, wrs)
        return it

    def wait_ge(self, s, v):
        return self._v.wait_ge(s, v)

    def drain(self):
        self._recent.clear()
        return self._v.drain()

    def memset(self, ap, c):
        return self._emit(self._v.memset, [ap], [], ap, c)

    def tensor_copy(self, out, in_):
        return self._emit(self._v.tensor_copy, [out], [in_], out, in_)

    def tensor_tensor(self, out, a, b, op):
        return self._emit(self._v.tensor_tensor, [out], [a, b], out, a, b, op)

    def tensor_add(self, out, a, b):
        return self.tensor_tensor(out, a, b, ALU.add)

    def tensor_sub(self, out, a, b):
        return self.tensor_tensor(out, a, b, ALU.subtract)

    def tensor_mul(self, out, a, b):
        return self.tensor_tensor(out, a, b, ALU.mult)

    def tensor_max(self, out, a, b):
        return self.tensor_tensor(out, a, b, ALU.max)

    def tensor_scalar(self, out, in0, s1, s2, op0, op1=ALU.bypass, **kw):
        ins = [in0]
        if isinstance(s1, bass.AP):
            ins.append(s1)
        if isinstance(s2, bass.AP):
            ins.append(s2)
        ao = kw.get("accum_out")
        outs = [out] + ([ao] if ao is not None else [])
        return self._emit(self._v.tensor_scalar, outs, ins,
                          out, in0, s1, s2, op0, op1, **kw)

    def tensor_single_scalar(self, out, in_, s, op):
        ins = [in_] + ([s] if isinstance(s, bass.AP) else [])
        return self._emit(self._v.tensor_single_scalar, [out], ins,
                          out, in_, s, op)

    def scalar_tensor_tensor(self, out, in0, s, in1, op0, op1, **kw):
        ins = [in0, in1] + ([s] if isinstance(s, bass.AP) else [])
        ao = kw.get("accum_out")
        outs = [out] + ([ao] if ao is not None else [])
        return self._emit(self._v.scalar_tensor_tensor, outs, ins,
                          out, in0, s, in1, op0, op1, **kw)

    def reduce_sum(self, out, in_, **kw):
        return self._emit(self._v.reduce_sum, [out], [in_], out, in_, **kw)

    def reduce_max(self, out, in_, **kw):
        return self._emit(self._v.reduce_max, [out], [in_], out, in_, **kw)

    def reciprocal(self, out, in_):
        return self._emit(self._v.reciprocal, [out], [in_], out, in_)


_MISSING = set()


class StopEmission(Exception):
    pass


def _stage_guard(fn, epilogue=None, fin=None):
    def wrapped(eng):
        try:
            fn(eng)
        except StopEmission:
            pass
        if fin is not None:
            fin(eng)
        if epilogue is not None:
            epilogue(eng)
    return wrapped


def build_nc(tick, plan):
    NT1B = min(NT1, TLIM)
    """Build the SPMD program.  plan=True: collect semaphore milestones into
    tick (waits on unknown events are skipped).  plan=False: emit with real
    waits from a completed tick."""
    nc = bass.Bass("TRN2", target_bir_lowering=False)

    x_in = nc.dram_tensor("x_in", [TOK, D], F32, kind="ExternalInput")
    x_halo = nc.dram_tensor("x_halo", [1, D], F32, kind="ExternalInput")
    Wq = nc.dram_tensor("Wq", [D, D], F32, kind="ExternalInput")
    Wv = nc.dram_tensor("Wv", [D, D], F32, kind="ExternalInput")
    Wo = nc.dram_tensor("Wo", [D, D], F32, kind="ExternalInput")
    bq_d = nc.dram_tensor("bq", [1, D], F32, kind="ExternalInput")
    bv_d = nc.dram_tensor("bv", [1, D], F32, kind="ExternalInput")
    bo_d = nc.dram_tensor("bo", [1, D], F32, kind="ExternalInput")
    w1T = nc.dram_tensor("w1T", [D, DHID], F32, kind="ExternalInput")
    w2T = nc.dram_tensor("w2T", [DHID, D], F32, kind="ExternalInput")
    cw_d = nc.dram_tensor("cw", [3, DHID], F32, kind="ExternalInput")
    cb_d = nc.dram_tensor("cb", [1, DHID], F32, kind="ExternalInput")
    par_d = nc.dram_tensor("par", [1, 2], F32, kind="ExternalInput")

    y_out = nc.dram_tensor("y_out", [TOK, D], F32, kind="ExternalOutput")

    xln_hbm = nc.dram_tensor("xln_hbm", [TOK, D], BF16)
    q_hbm = nc.dram_tensor("q_hbm", [TOK, D], BF16)
    x1_hbm = nc.dram_tensor("x1_hbm", [TOK, D], F32)
    xq_hbm = nc.dram_tensor("xq_hbm", [XQW, D], BF16)
    wq1_hbm = nc.dram_tensor("wq1_hbm", [32 - W1RES, 128, 8, 128], BF16)
    wq2_hbm = nc.dram_tensor("wq2_hbm", [32, 128, D], BF16)
    cc_in = nc.dram_tensor("cc_in", [67, D], F32)
    cc_out = nc.dram_tensor("cc_out", [67, D], F32)
    hrow_hbm = nc.dram_tensor("hrow_hbm", [1, D], F32)
    ident_d = nc.inline_tensor(np.eye(128, dtype=np.float32), name="ident")
    if DBG:
        dbg_q = nc.dram_tensor("dbg_q", [TOK, D], BF16, kind="ExternalOutput")
        dbg_xln = nc.dram_tensor("dbg_xln", [TOK, D], BF16,
                                 kind="ExternalOutput")
        dbg_wq2 = nc.dram_tensor("dbg_wq2", [32, 128, D], BF16,
                                 kind="ExternalOutput")
        dbg_st = nc.dram_tensor("dbg_st", [128, 24], F32,
                                 kind="ExternalOutput")
        dbg_cc = nc.dram_tensor("dbg_cc", [67, D], F32, kind="ExternalOutput")
        dbg_M = nc.dram_tensor("dbg_M", [128, 8192], BF16,
                               kind="ExternalOutput")
        dbg_x1 = nc.dram_tensor("dbg_x1", [TOK, D], F32, kind="ExternalOutput")
        dbg_xqt = nc.dram_tensor("dbg_xqt", [XQW, D], BF16,
                                 kind="ExternalOutput")
        dbg_h = nc.dram_tensor("dbg_h", [128, 16384], F32,
                               kind="ExternalOutput")
        dbg_w1 = nc.dram_tensor("dbg_w1", [128, W1RES * 8 * 128], BF16,
                                kind="ExternalOutput")

    with ExitStack() as st:
        e = st.enter_context

        arena_a = e(nc.sbuf_tensor("arena_a", [128, 16384], F32))
        aa_bf = arena_a[:].bitcast(BF16)
        Wq16 = aa_bf[:, 0:8192]
        Wv16 = aa_bf[:, 8192:16384]
        Wo16h = aa_bf[:, 16384:32768]
        harena = arena_a
        ha_bf = aa_bf

        arena_b = e(nc.sbuf_tensor("arena_b", [128, 8192], F32))
        ab_bf = arena_b[:].bitcast(BF16)
        wbuf = [arena_b[:, 0:4096], arena_b[:, 4096:8192]]
        M16 = ab_bf[:, 0:8192]
        xqTw = ab_bf[:, 8192:12288]
        xqTw2 = ab_bf[:, 12288:16384]

        wqtmp = arena_a[:, 8192:12288]
        wqout = aa_bf[:, 24576:32768]
        w1res = e(nc.sbuf_tensor("w1res", [128, W1RES * 8 * 128], BF16))
        w1st = [e(nc.sbuf_tensor(f"w1st{i}", [128, 1024], BF16))
                for i in range(2)]
        w2st = [e(nc.sbuf_tensor(f"w2st{i}", [128, 1024], BF16))
                for i in range(2)]

        x_sb = [e(nc.sbuf_tensor(f"x_sb{i}", [128, 1024], F32))
                for i in range(2)]
        xln_sb = [e(nc.sbuf_tensor(f"xln_sb{i}", [128, 1024], BF16))
                  for i in range(2)]
        xlt_sb = [e(nc.sbuf_tensor(f"xlt_sb{i}", [128, 1024], BF16))
                  for i in range(2)]
        e_sb = [e(nc.sbuf_tensor(f"e_sb{i}", [128, 1024], F32))
                for i in range(2)]
        q16_sb = [e(nc.sbuf_tensor(f"q16_sb{i}", [128, 1024], BF16))
                  for i in range(2)]
        eq_sb = [e(nc.sbuf_tensor(f"eq_sb{i}", [128, 1024], BF16))
                 for i in range(2)]
        vaug_sb = [e(nc.sbuf_tensor(f"vaug_sb{i}", [128, 1088], BF16))
                   for i in range(2)]
        x1_sb = [e(nc.sbuf_tensor(f"x1_sb{i}", [128, 1024], F32))
                 for i in range(2)]
        xq_sb = [e(nc.sbuf_tensor(f"xq_sb{i}", [128, 1024], BF16))
                 for i in range(2)]
        x1r_sb = [e(nc.sbuf_tensor(f"x1r_sb{i}", [128, 1024], F32))
                  for i in range(2)]
        qt_sb = eq_sb
        ysc_sb = [xln_sb[0][:].bitcast(F32), xln_sb[1][:].bitcast(F32)]
        usc_sb = [xlt_sb[0][:].bitcast(F32), xlt_sb[1][:].bitcast(F32)]
        scrS = e(nc.sbuf_tensor("scrS", [128, 1024], F32))
        scrW = e_sb[0][:]
        scrR = e_sb[1][:]
        scr2 = e(nc.sbuf_tensor("scr2", [128, 1024], F32))
        amx_sb = q16_sb[0][:].bitcast(F32)[:, 0:512]
        srow_sb = e(nc.sbuf_tensor("srow", [1, 512], F32))
        srep_sb = q16_sb[1][:].bitcast(F32)[:, 0:512]
        ctx_sb = x_sb[0][0:65, :]
        ctx16_sb = x_sb[1][:].bitcast(BF16)[0:64, 0:1024]
        sfac_sb = e(nc.sbuf_tensor("sfac", [128, 8], F32))
        amn_sb = e(nc.sbuf_tensor("amn", [128, 512], F32))
        wo32_sb = [x1_sb[0][0:64, :], x1_sb[1][0:64, :]]

        ones16 = e(nc.sbuf_tensor("ones16", [1, 128], BF16))
        ones32c = e(nc.sbuf_tensor("ones32c", [128, 1], F32))
        ones32r = e(nc.sbuf_tensor("ones32r", [1, 128], F32))
        id32 = e(nc.sbuf_tensor("id32", [128, 128], F32))
        allones = e(nc.sbuf_tensor("allones", [128, 128], F32))
        bqv16 = e(nc.sbuf_tensor("bqv16", [1, 3072], BF16))
        cw_sb = e(nc.sbuf_tensor("cw_sb", [128, 128], F32))
        st1 = e(nc.sbuf_tensor("st1", [128, 32], F32))
        st2 = e(nc.sbuf_tensor("st2", [128, 32], F32))
        wst = e(nc.sbuf_tensor("wst", [128, 16], F32))
        srow1 = e_sb[1][0:1, :]
        srow2 = scr2[0:1, :]
        srow3 = e_sb[0][0:1, :]
        qh16 = e(nc.sbuf_tensor("qh16", [128, 8], BF16))
        par_sb = e(nc.sbuf_tensor("par_sb", [1, 2], F32))
        wsc_sb = e(nc.sbuf_tensor("wsc_sb", [128, 8], F32))
        cpool = e(nc.sbuf_tensor("cpool", [128, 4], F32))
        jnk = e(nc.sbuf_tensor("jnk", [1, 512], F32))
        _CONSTS = [EPS, MAGIC, -MAGIC, -0.5]
        for _ci, _cv in enumerate(_CONSTS):
            nc.const_aps.register(F32, _cv, cpool[:, _ci:_ci + 1])

        P_A = e(nc.psum_tensor("P_A", [128, 1024], F32))
        P_B = e(nc.psum_tensor("P_B", [128, 1024], F32))
        P_C = e(nc.psum_tensor("P_C", [128, 1024], F32))
        P_D = e(nc.psum_tensor("P_D", [128, 1024], F32))
        PQ = {0: P_C[:, 0:512], 1: P_C[:, 512:1024],
              2: P_D[:, 0:512], 3: P_D[:, 512:1024]}
        PT = [P_A, P_B, P_C, P_D]

        sV = e(nc.semaphore("sV"))
        sS = e(nc.semaphore("sS"))
        sP = e(nc.semaphore("sP"))
        sG = e(nc.semaphore("sG"))
        sCC = e(nc.semaphore("sCC"))
        sFN = e(nc.semaphore("sFN"))
        sFNS = e(nc.semaphore("sFNS"))
        sFNV = e(nc.semaphore("sFNV"))
        sFNG = e(nc.semaphore("sFNG"))
        sem = {"sV": sV, "sS": sS, "sP": sP, "sG": sG, "sFN": sFN,
               "sFNS": sFNS, "sFNV": sFNV, "sFNG": sFNG}
        cnt = {"sV": 0, "sS": 0, "sP": 0, "sG": 0, "sFN": 0,
               "sFNS": 0, "sFNV": 0, "sFNG": 0}
        _DSTREAMS = ["c0", "xa0", "xa1", "lnst", "xlt0", "xlt1", "qst",
                     "p2", "wo0", "wo1", "qt0", "qt1", "x30", "x31",
                     "x1s", "spill", "wxq", "xr0", "xr1", "yo",
                     "abi", "awl", "aml", "aql", "aqs",
                     "gbq", "gqr", "gcc", "gh", "gw1a", "gw1b", "aqlb", "amlb",
                     "gw2a", "gw2b"]
        for _sn in _DSTREAMS:
            sem[_sn] = e(nc.semaphore("sem_" + _sn))
            cnt[_sn] = 0
        blk = e(nc.Block())

        def inc(instr, name, n=1):
            instr.then_inc(sem[name], n)
            cnt[name] += n

        def setev(name):
            def _set(evname, semname):
                tick[evname] = (semname, cnt[semname])
            return _set

        def ev(name, semname):
            tick[name] = (semname, cnt[semname])

        def mkdma(eng):
            def sdma(stream, out, in_, gate=False, **kw):
                if gate and cnt[stream] > 0:
                    eng.wait_ge(sem[stream], cnt[stream])
                it = eng.dma_start(out=out, in_=in_, **kw)
                it.then_inc(sem[stream], 16)
                cnt[stream] += 16
                return cnt[stream]
            return sdma

        def wts(eng, *names):
            best = {}
            for name in names:
                if name is None:
                    continue
                if name not in tick:
                    if not plan:
                        _MISSING.add(name)
                    continue
                s, v = tick[name]
                if v > 0:
                    best[s] = max(best.get(s, 0), v)
            for s, v in best.items():
                eng.wait_ge(sem[s], v)

        wt = wts

        def _sync_epilogue(sync):
            def dma(out, in_, **kw):
                it = sync.dma_start(out=out, in_=in_, **kw)
                it.then_inc(sem["yo"], 16)
                cnt["yo"] += 16

            if DBG:
                wt(sync, "vector_done")
                wt(sync, "scalar_done")
                wt(sync, "gp_done")
                wts(sync, *["__tot_" + s for s in _DSTREAMS if s != "yo"])
                if STAGE >= 1:
                    for t in range(NT1):
                        dma(dbg_xln[t * 128:(t + 1) * 128, :],
                            xln_hbm[t * 128:(t + 1) * 128, :])
                if STAGE >= 2:
                    dma(dbg_st[:, 0:8], wsc_sb[:])
                    dma(dbg_st[:, 8:14], wst[:, 2:8])
                    dma(dbg_st[:, 16:24], st2[:, 16:24])
                    dma(dbg_w1[:, :], w1res[:, 0:W1RES * 8 * 128])
                    for ch in range(32):
                        dma(dbg_wq2[ch], wq2_hbm[ch])
                if STAGE >= 3:
                    for t in range(NT1B):
                        dma(dbg_q[t * 128:(t + 1) * 128, :],
                            q_hbm[t * 128:(t + 1) * 128, :])
                    dma(dbg_cc[:, :],
                        cc_in[:, :] if STAGE < 4 else cc_out[:, :])
                if STAGE >= 4:
                    dma(dbg_M[:, :], M16)
                if STAGE >= 5:
                    for t in range(NT1):
                        dma(dbg_x1[t * 128:(t + 1) * 128, :],
                            x1_hbm[t * 128:(t + 1) * 128, :])
                    for t in range(XQW // 128):
                        dma(dbg_xqt[t * 128:(t + 1) * 128, :],
                            xq_hbm[t * 128:(t + 1) * 128, :])
                if STAGE >= 6:
                    dma(dbg_h[:, :], harena[:])
            wts(sync, *["__tot_" + s for s in _DSTREAMS if s != "yo"])
            if cnt["yo"] > 0:
                sync.wait_ge(sem["yo"], cnt["yo"])

        def _sync_body(sync):
            sdma = mkdma(sync)

            sdma("c0", par_sb[:], par_d[:, :])
            for i, src in enumerate([cw_d[0:1, :], cw_d[1:2, :],
                                     cw_d[2:3, :], cb_d[0:1, :]]):
                sdma("c0", cw_sb[:, i * 32:(i + 1) * 32],
                     src.rearrange("r (cc p) -> p (r cc)", p=128),
                     allow_slow_non_contiguous=True)
            ev("c_small", "c0")

            for t in range(NT1):
                if t >= 2:
                    wts(sync, f"1a_xln_{t-2}", f"1a_sq_{t-2}")
                xs = "xa0" if t % 2 == 0 else "xa1"
                sdma(xs, x_sb[t % 2][:], x_in[t * 128:(t + 1) * 128, :])
                ev(f"1a_x_{t}", xs)
                wt(sync, f"1a_xln_{t}")
                sdma("lnst", xln_hbm[t * 128:(t + 1) * 128, :],
                     xln_sb[t % 2][:], gate=True)
                ev(f"1a_st_{t}", "lnst")

            for t in range(NT1B if STAGE >= 3 else 0):
                if t >= 2:
                    wt(sync, f"1b_pe_v_{t-2}")
                wt(sync, f"1a_st_{t}")
                xls = "xlt0" if t % 2 == 0 else "xlt1"
                for g in range(8):
                    sdma(xls, xlt_sb[t % 2][:, g * 128:(g + 1) * 128],
                         xln_hbm[t * 128:(t + 1) * 128,
                                 g * 128:(g + 1) * 128],
                         transpose=True)
                ev(f"1b_xlt_{t}", xls)
                wt(sync, f"1b_q16_{t}")
                sdma("qst", q_hbm[t * 128:(t + 1) * 128, :], q16_sb[t % 2][:],
                     gate=True)
                ev(f"1b_qst_{t}", "qst")

            if STAGE < 4:
                raise StopEmission
            sync.wait_ge(sCC, 1)
            sdma("p2", ctx_sb, cc_out[0:65, :])
            sdma("p2", sfac_sb[:], cc_out[64:65, :].rearrange(
                "r (a two d) -> (two d) (r a)", a=8, two=2, d=64),
                allow_slow_non_contiguous=True)
            wt(sync, "v_bias")
            sdma("p2", srow1, cc_out[65:66, :])
            sdma("p2", srow2, cc_out[66:67, :])
            sdma("p2", srow3, x_halo[:, :])
            ev("p2_ld", "p2")
            for h in range(H):
                if h >= 2:
                    wt(sync, f"p2_wo_v_{h-2}")
                ws = "wo0" if h % 2 == 0 else "wo1"
                sdma(ws, wo32_sb[h % 2], Wo[h * 64:(h + 1) * 64, :])
                ev(f"p2_wo_{h}", ws)

            if STAGE < 5:
                raise StopEmission
            wt(sync, "g_pad")
            for phase_idx, tiles in enumerate(A_SCHED):
                for t in tiles:
                    if t >= 2:
                        wts(sync, f"a_pe_{t-2}")
                    else:
                        wts(sync, "p2_halo_pe", "p2_ctx16",
                            f"p2_wo_v_{H-1}")
                    qs = "qt0" if t % 2 == 0 else "qt1"
                    x3 = "x30" if t % 2 == 0 else "x31"
                    for g in range(8):
                        sdma(qs, qt_sb[t % 2][:, g * 128:(g + 1) * 128],
                             q_hbm[t * 128:(t + 1) * 128,
                                   g * 128:(g + 1) * 128],
                             transpose=True)
                    if t >= 2:
                        wt(sync, f"a_x1_{t-2}")
                    sdma(x3, x_sb[t % 2][:], x_in[t * 128:(t + 1) * 128, :])
                    ev(f"a_ld_{t}", x3)
                    ev(f"a_qt_{t}", qs)
                    wt(sync, f"a_x1_{t}")
                    sdma("x1s", x1_hbm[t * 128:(t + 1) * 128, :],
                         x1_sb[t % 2][:], gate=True)
                    ev(f"a_x1s_{t}", "x1s")
                    wt(sync, f"a_xq_{t}")
                    sdma("spill", xq_hbm[1 + 128 * t:1 + 128 * (t + 1), :],
                         xq_sb[t % 2][:], gate=True)
                    ev(f"a_spill_{t}", "spill")
                if phase_idx == 0:
                    continue
                g = phase_idx - 1
                if STAGE < 6 or g >= WLIM:
                    continue
                W0 = g * GW
                ntt = _ceil(GOUT[g], 128)
                if g >= 1:
                    wt(sync, f"w_w1done_{g-1}")
                wt(sync, f"a_spill_{min(_ceil(W0 + 512, 128), NT1) - 1}")
                for k in range(8):
                    dstw = xqTw if k < 4 else xqTw2
                    sdma("wxq", dstw[:, (k % 4) * 512:(k % 4 + 1) * 512],
                         xq_hbm[W0:W0 + 512, k * 128:(k + 1) * 128],
                         transpose=True)
                ev(f"w_xqT_{g}", "wxq")
                for ch in range(32):
                    w2s = "gw2a" if ch % 2 == 0 else "gw2b"
                    if g == 0:
                        wt(sync, f"wqst_1_{ch}")
                    if ch >= 2:
                        wt(sync, f"w_pe_w2_{g}_{ch-2}")
                    elif g > 0:
                        wt(sync, f"w_pe_w2_{g-1}_{30 + ch}")
                    sdma(w2s, w2st[ch % 2][:], wq2_hbm[ch])
                    ev(f"w2s_{g}_{ch}", w2s)

                def _ost(tt):
                    r0 = W0 + tt * 128
                    pr = min(128, W0 + GOUT[g] - r0)
                    wt(sync, f"w_out_{g}_{tt}")
                    sdma("yo", y_out[r0:r0 + pr, :], x1r_sb[tt % 2][0:pr, :])
                    ev(f"w_ost_{g}_{tt}", "yo")

                ntt_prev = _ceil(GOUT[g - 1], 128) if g >= 1 else 0
                for tt in range(ntt):
                    r0 = W0 + tt * 128
                    pr = min(128, W0 + GOUT[g] - r0)
                    if tt >= 2:
                        _ost(tt - 2)
                        wt(sync, f"w_ost_{g}_{tt-2}")
                    elif g >= 1:
                        wt(sync, f"w_ost_{g-1}_{ntt_prev-1}")
                    wt(sync, f"a_x1s_{min(_ceil(r0 + pr, 128), NT1) - 1}")
                    xr = "xr0" if tt % 2 == 0 else "xr1"
                    sdma(xr, x1r_sb[tt % 2][0:pr, :], x1_hbm[r0:r0 + pr, :])
                    ev(f"w_x1r_{g}_{tt}", xr)
                for tt in range(max(0, ntt - 2), ntt):
                    _ost(tt)
            raise StopEmission

        blk.sync(_stage_guard(_sync_body, _sync_epilogue))

        def _fin_scalar(eng):
            inc(eng.nop(), "sFNS")
            ev("scalar_done", "sFNS")

        def _fin_vector(eng):
            inc(eng.nop() if hasattr(eng, "nop") else eng._v.nop(), "sFNV")
            ev("vector_done", "sFNV")

        def _fin_gp(eng):
            inc(eng.nop(), "sFNG")
            ev("gp_done", "sFNG")

        def _fin_tensor(eng):
            inc(eng.nop(), "sFN")
            ev("tensor_done", "sFN")

        def _scalar_body(scalar):
            scalar = _ScalGuard(scalar)
            sdma = mkdma(scalar)

            def incd(_it, n=1):
                it2 = scalar.drain()
                it2.then_inc(sem["sS"], n)
                cnt["sS"] += n

            def sdelay(n=3):
                for _ in range(n):
                    scalar.activation(jnk[0:1, 256:384], jnk[0:1, 384:512],
                                      AF.Copy)

            def act(out, in_, func, **kw):
                return scalar.activation(out, in_, func, **kw)

            scalar.wait_ge(sV, 1)
            sdma("abi", srow1, bv_d[:, :])
            sdma("abi", srow2, bo_d[:, :])
            ev("bias_ld", "abi")

            wldix = [0]
            for wname, wt_d in enumerate([Wq, Wv]):
                for gch in range(8):
                    i = wldix[0]
                    if i >= 2:
                        pi = i - 2
                        wt(scalar, f"wcast_{pi // 8}_{pi % 8}")
                    sdma("awl", wbuf[gch % 2][:, 0:1024],
                         wt_d[gch * 128:(gch + 1) * 128, :], gate=True)
                    ev(f"wld_{wname}_{gch}", "awl")
                    wldix[0] += 1

            wt(scalar, "v_bias")
            for t in range(NT1):
                wts(scalar, f"1a_x_{t}",
                    f"1a_var_{t-1}" if t >= 1 else None)
                it = act(scrS[:], x_sb[t % 2][:], AF.Square,
                         accum_out=st1[:, 4:5])
                incd(it)
                ev(f"1a_sq_{t}", "sS")
                wts(scalar, f"1a_var_{t}",
                    f"1a_xln_{t-1}" if t >= 1 else None)
                act(st1[:, 8:9], st1[:, 3:4], AF.Ln, bias=EPS)
                it = act(st1[:, 6:7], st1[:, 8:9], AF.Exp, scale=-0.5)
                incd(it)
                ev(f"1a_rstd_{t}", "sS")

            if STAGE < 2:
                raise StopEmission
            mseq = [(0, c) for c in range(8)] + [(1, c) for c in range(32)]
            for i, (wi, ch) in enumerate(mseq):
                wt_d = w1T if wi == 0 else w2T
                width = DHID if wi == 0 else D
                if i == 0:
                    wt(scalar, "wcast_1_7")
                elif i == 1:
                    wt(scalar, "wcast_1_6")
                else:
                    pwi, pch = mseq[i - 2]
                    wt(scalar, f"wmean_{pwi}_{pch}")
                sdma("aml" if ch % 2 == 0 else "amlb",
                     wbuf[ch % 2][:, 0:width],
                     wt_d[ch * 128:(ch + 1) * 128, :])
                ev(f"wmld_{wi}_{ch}", "aml" if ch % 2 == 0 else "amlb")
            def _s_wq_rounds():
                qseq = [(0, c) for c in range(8)] + [(1, c) for c in range(32)]

                def _ld(i):
                    wi, ch = qseq[i]
                    wt_d = w1T if wi == 0 else w2T
                    width = DHID if wi == 0 else D
                    if i < 2:
                        wt(scalar, "wmean_1_31" if i == 0 else "wmean_1_30")
                        wt(scalar, "wsc_ready")
                    else:
                        pwi, pch = qseq[i - 2]
                        wt(scalar, f"wq_v_{pwi}_{pch}")
                    sdma("aql" if ch % 2 == 0 else "aqlb",
                         wbuf[ch % 2][:, 0:width],
                         wt_d[ch * 128:(ch + 1) * 128, :])
                    ev(f"wqld_{wi}_{ch}", "aql" if ch % 2 == 0 else "aqlb")

                _ld(0)
                for i, (wi, ch) in enumerate(qseq):
                    width = DHID if wi == 0 else D
                    if i + 1 < len(qseq):
                        _ld(i + 1)
                    wt(scalar, f"wqld_{wi}_{ch}")
                    it = act(wqtmp[:, 0:width], wbuf[ch % 2][:, 0:width],
                             AF.Identity, scale=wsc_sb[:, wi:wi + 1],
                             bias=MAGIC)
                    inc(it, "sS")
                    ev(f"wq_aff_{wi}_{ch}", "sS")
                    wt(scalar, f"wq_v_{wi}_{ch}")
                    if wi == 0:
                        gatefirst = True
                        for cc in range(W1RES, 32):
                            sdma("aqs", wq1_hbm[cc - W1RES, :, ch, :],
                                 wqout[:, cc * 128:(cc + 1) * 128],
                                 gate=gatefirst)
                            gatefirst = False
                    else:
                        sdma("aqs", wq2_hbm[ch, :, :], wqout[:, 0:1024],
                             gate=True)
                    ev(f"wqst_{wi}_{ch}", "aqs")

            if STAGE == 2:
                _s_wq_rounds()

            if STAGE < 3:
                raise StopEmission
            for t in range(NT1B):
                wt(scalar, f"1b_pe_q_{t}")
                if t >= 2:
                    wt(scalar, f"1b_q16_{t-2}")
                sdelay(1)
                PQB = P_B if t % 2 == 0 else P_D
                act(e_sb[t % 2][:, 0:512], PQB[:, 0:512], AF.Exp)
                it = act(e_sb[t % 2][:, 512:1024], PQB[:, 512:1024], AF.Exp)
                inc(it, "sS")
                ev(f"1b_s_exp_{t}", "sS")
                wt(scalar, f"1b_pe_vmm_{t}")
                sdelay(1)
                va = vaug_sb[t % 2][:].rearrange("p (h s) -> p h s", s=68)
                act(va[:, 0:8, 0:64],
                    P_C[:, 0:512].rearrange("p (h d) -> p h d", d=64),
                    AF.Copy)
                it = act(va[:, 8:16, 0:64],
                         P_C[:, 512:1024].rearrange("p (h d) -> p h d", d=64),
                         AF.Copy)
                inc(it, "sS")
                ev(f"1b_s_v_{t}", "sS")
                if t >= 1:
                    wt(scalar, f"1b_q16_{t-1}")
                    it = act(eq_sb[(t - 1) % 2][:], q16_sb[(t - 1) % 2][:],
                             AF.Exp)
                    inc(it, "sS")
                    ev(f"1b_s_eq_{t-1}", "sS")
            wt(scalar, f"1b_q16_{NT1B-1}")
            it = act(eq_sb[(NT1B - 1) % 2][:], q16_sb[(NT1B - 1) % 2][:],
                     AF.Exp)
            inc(it, "sS")
            ev(f"1b_s_eq_{NT1B-1}", "sS")
            _s_wq_rounds()

            if STAGE < 4:
                raise StopEmission
            wt(scalar, "p2_halo_pe")
            sdelay()
            act(scrS[0:1, :], P_B[0:1, :], AF.Square,
                accum_out=st1[0:1, 0:1])
            it = act(scrS[0:1, :], P_B[0:1, :], AF.Copy,
                     accum_out=st1[0:1, 2:3])
            incd(it)
            ev("p2_s_stats", "sS")
            wt(scalar, "p2_v_var")
            act(st1[0:1, 4:5], st1[0:1, 3:4], AF.Ln, bias=EPS)
            it = act(st1[0:1, 5:6], st1[0:1, 4:5], AF.Exp, scale=-0.5)
            incd(it)
            ev("p2_s_rstd", "sS")
            wt(scalar, "p2_v_x1h")
            act(scrS[0:1, :], srow3, AF.Square, accum_out=st1[0:1, 8:9])
            it = act(scrS[0:1, :], srow3, AF.Copy,
                     accum_out=st1[0:1, 10:11])
            incd(it)
            ev("p2_s_stats2", "sS")
            wt(scalar, "p2_v_var2")
            act(st1[0:1, 12:13], st1[0:1, 11:12], AF.Ln, bias=EPS)
            it = act(st1[0:1, 13:14], st1[0:1, 12:13], AF.Exp, scale=-0.5)
            incd(it)
            ev("p2_s_rstd2", "sS")

            if STAGE < 5:
                raise StopEmission
            prev_t = [None]
            for phase_idx, tiles in enumerate(A_SCHED):
                for t in tiles:
                    psA = P_A if t % 2 == 0 else P_B
                    wts(scalar, f"a_pe_{t}",
                        None if prev_t[0] is None else
                        f"a_v_bc_{prev_t[0]}")
                    prev_t[0] = t
                    sdelay(1)
                    act(scrS[:, 0:512], psA[:, 0:512], AF.Copy,
                        accum_out=st1[:, 0:1])
                    act(scrS[:, 512:1024], psA[:, 512:1024], AF.Copy,
                        accum_out=st1[:, 1:2])
                    act(scrS[:, 0:512], psA[:, 0:512], AF.Square,
                        accum_out=st1[:, 2:3])
                    it = act(scrS[:, 512:1024], psA[:, 512:1024], AF.Square,
                             accum_out=st1[:, 3:4])
                    incd(it)
                    ev(f"a_s_st_{t}", "sS")
                    wt(scalar, f"a_v_var_{t}")
                    act(st1[:, 8:9], st1[:, 6:7], AF.Ln, bias=EPS)
                    it = act(st1[:, 9:10], st1[:, 8:9], AF.Exp, scale=-0.5)
                    incd(it)
                    ev(f"a_s_rstd_{t}", "sS")
                    wt(scalar, f"a_x1_{t}")
                    it = act(scrS[:], x1_sb[t % 2][:], AF.Square,
                             accum_out=st2[:, 0:1])
                    incd(it)
                    ev(f"a_s_sq2_{t}", "sS")
                    wt(scalar, f"a_v_var2_{t}")
                    act(st2[:, 4:5], st2[:, 3:4], AF.Ln, bias=EPS)
                    it = act(st2[:, 5:6], st2[:, 4:5], AF.Exp, scale=-0.5)
                    incd(it)
                    ev(f"a_s_rstd2_{t}", "sS")
                    wt(scalar, f"a_v_nm2_{t}")
                    it = act(scr2[:], x1_sb[t % 2][:], AF.Identity,
                             scale=st2[:, 5:6], bias=st2[:, 6:7])
                    inc(it, "sS")
                    ev(f"a_s_xln2_{t}", "sS")
                    wt(scalar, f"a_v_sc_{t}")
                    it = act(scrR[:], scr2[:], AF.Identity,
                             scale=st2[:, 8:9], bias=MAGIC)
                    inc(it, "sS")
                    ev(f"a_s_rt_{t}", "sS")
                    wt(scalar, f"a_v_bc_{t}")
                    it = act(xq_sb[t % 2][:], scrR[:], AF.Identity,
                             scale=st2[:, 10:11], bias=st2[:, 11:12])
                    inc(it, "sS")
                    ev(f"a_xq_{t}", "sS")
                if phase_idx == 0:
                    continue
                g = phase_idx - 1
                if STAGE < 6 or g >= WLIM:
                    continue
                W0 = g * GW
                ntt = _ceil(GOUT[g], 128)
                for ch in range(32):
                    wts(scalar, f"w_pe_h_{g}_{ch}",
                        "c_small" if (g == 0 and ch == 0) else None)
                    it = act(harena[:, ch * 512:(ch + 1) * 512], PQ[ch % 4],
                             AF.Copy)
                    inc(it, "sS")
                    ev(f"w_hev_{g}_{ch}", "sS")
                    it = act(ysc_sb[ch % 2][:, 0:510],
                             harena[:, ch * 512:ch * 512 + 510],
                             AF.Identity, scale=cw_sb[:, ch:ch + 1],
                             bias=cw_sb[:, 96 + ch:97 + ch])
                    inc(it, "sS")
                    ev(f"w_cv1_{g}_{ch}", "sS")
                    wt(scalar, f"w_cv3_{g}_{ch}")
                    it = act(harena[:, ch * 512 + 1:ch * 512 + 511],
                             ysc_sb[ch % 2][:, 0:510], AF.Silu)
                    inc(it, "sS")
                    ev(f"w_silu_{g}_{ch}", "sS")
                    if ch >= 2:
                        wt(scalar, f"w_tr_{g}_{ch-2}")
                    it = act(usc_sb[ch % 2][:, 0:510],
                             harena[:, ch * 512 + 1:ch * 512 + 511], AF.Abs)
                    inc(it, "sS")
                    ev(f"w_abs_{g}_{ch}", "sS")
                def _s_rt(ch):
                    wt(scalar, f"w_u_{g}_{ch}")
                    it = act(ysc_sb[ch % 2][:, 0:510],
                             usc_sb[ch % 2][:, 0:510], AF.Identity,
                             bias=MAGIC)
                    inc(it, "sS")
                    ev(f"w_rt_{g}_{ch}", "sS")

                def _s_hq(ch):
                    wt(scalar, f"w_bc_{g}_{ch}")
                    it = act(ha_bf[:, ch * 1024 + 1:ch * 1024 + 511],
                             ysc_sb[ch % 2][:, 0:510], AF.Identity,
                             bias=-MAGIC)
                    inc(it, "sS")
                    ev(f"w_hq_{g}_{ch}", "sS")

                _s_rt(0)
                for ch in range(32):
                    if ch + 1 < 32:
                        _s_rt(ch + 1)
                    _s_hq(ch)
                for tt in range(ntt):
                    pr = min(128, GOUT[g] - tt * 128)
                    pst = PT[tt]
                    wts(scalar, f"w_pe_f_{g}",
                        f"w_v_fvar_{g}_{tt-1}" if tt >= 1 else None)
                    act(scrS[0:pr, 0:512], pst[0:pr, 0:512], AF.Copy,
                        accum_out=st1[0:pr, 16:17])
                    act(scrS[0:pr, 512:1024], pst[0:pr, 512:1024], AF.Copy,
                        accum_out=st1[0:pr, 17:18])
                    act(scrS[0:pr, 0:512], pst[0:pr, 0:512], AF.Square,
                        accum_out=st1[0:pr, 18:19])
                    it = act(scrS[0:pr, 512:1024], pst[0:pr, 512:1024],
                             AF.Square, accum_out=st1[0:pr, 19:20])
                    incd(it)
                    ev(f"w_s_fst_{g}_{tt}", "sS")
                    wt(scalar, f"w_v_fvar_{g}_{tt}")
                    act(st1[0:pr, 24:25], st1[0:pr, 23:24], AF.Ln, bias=EPS)
                    it = act(st1[0:pr, 25:26], st1[0:pr, 24:25], AF.Exp,
                             scale=-0.5)
                    incd(it)
                    ev(f"w_s_frstd_{g}_{tt}", "sS")


        blk.scalar(_stage_guard(_scalar_body, fin=_fin_scalar))

        def _vector_body(vector):
            vector = _VecGuard(vector)

            def incd(_it, n=1):
                it2 = vector.drain()
                it2.then_inc(sem["sV"], n)
                cnt["sV"] += n

            def vdelay(n=3):
                for _ in range(n):
                    vector.tensor_copy(jnk[0:1, 0:128], jnk[0:1, 128:256])


            def ts(out, in0, s1, s2, op0, op1=None, **kw):
                if op1 is None:
                    return vector.tensor_single_scalar(out, in0, s1, op0)
                return vector.tensor_scalar(out, in0, s1, s2, op0, op1, **kw)

            vector.memset(ones16[:], 1.0)
            vector.memset(ones32c[:], 1.0)
            for _ci, _cv in enumerate(_CONSTS):
                vector.memset(cpool[:, _ci:_ci + 1], _cv)
            vector.memset(ones32r[:], 1.0)
            vector.memset(allones[:], 1.0)
            vector.memset(jnk[:], 0.0)
            if DBG:
                vector.memset(wst[:], 0.0)
                vector.memset(st2[:], 0.0)
            for pbuf in range(2):
                va = vaug_sb[pbuf][:].rearrange("p (h s) -> p h s", s=68)
                vector.memset(va[:, :, 64:65], 1.0)
            it = vector.tensor_copy(st2[0:1, 31:32], ones32c[0:1, 0:1])
            inc(it, "sV")
            ev("v_const", "sV")

            wt(vector, "g_bq")
            vector.tensor_copy(bqv16[:, 0:1024], srow3)
            wt(vector, "bias_ld")
            vector.tensor_copy(bqv16[:, 1024:2048], srow1)
            it = vector.tensor_copy(bqv16[:, 2048:3072], srow2)
            inc(it, "sV")
            ev("v_bias", "sV")

            for wname in range(2):
                dst16 = Wq16 if wname == 0 else Wv16
                for gch in range(8):
                    wt(vector, f"wld_{wname}_{gch}")
                    it = vector.tensor_copy(
                        dst16[:, gch * 1024:(gch + 1) * 1024],
                        wbuf[gch % 2][:, 0:1024])
                    inc(it, "sV")
                    ev(f"wcast_{wname}_{gch}", "sV")

            for t in range(NT1):
                wt(vector, f"1a_x_{t}")
                vector.reduce_sum(st1[:, 0:1], x_sb[t % 2][:], axis=AX.X)
                ts(st1[:, 1:2], st1[:, 0:1], 1.0 / D, None, ALU.mult)
                vector.tensor_mul(st1[:, 2:3], st1[:, 1:2], st1[:, 1:2])
                wt(vector, f"1a_sq_{t}")
                it = ts(st1[:, 3:4], st1[:, 4:5], 1.0 / D, st1[:, 2:3],
                        ALU.mult, ALU.subtract)
                incd(it)
                ev(f"1a_var_{t}", "sV")
                wt(vector, f"1a_rstd_{t}")
                it = ts(xln_sb[t % 2][:], x_sb[t % 2][:], st1[:, 1:2],
                        st1[:, 6:7], ALU.subtract, ALU.mult)
                inc(it, "sV")
                ev(f"1a_xln_{t}", "sV")

            if STAGE < 2:
                raise StopEmission
            for wi, (nch, width) in enumerate([(8, DHID), (32, D)]):
                for ch in range(nch):
                    wt(vector, f"wmld_{wi}_{ch}")
                    vector.reduce_sum(wst[:, 0:1], wbuf[ch % 2][:, 0:width],
                                      axis=AX.X, apply_absolute_value=True)
                    if DBG and wi == 0:
                        vector.tensor_copy(wst[:, 4 + ch:5 + ch],
                                           wst[:, 0:1])
                    if ch == 0:
                        it = vector.tensor_copy(wst[:, 2 + wi:3 + wi],
                                                wst[:, 0:1])
                    else:
                        it = vector.tensor_add(wst[:, 2 + wi:3 + wi],
                                               wst[:, 2 + wi:3 + wi],
                                               wst[:, 0:1])
                    incd(it)
                    ev(f"wmean_{wi}_{ch}", "sV")
            ev("wmean_all", "sV")
            wt(vector, "w_srep_pe")
            vdelay()
            ts(wsc_sb[:, 6:8], P_D[:, 16:18], 1.0 / (D * DHID), None,
               ALU.mult)
            if DBG:
                vector.tensor_copy(st2[:, 22:24], wsc_sb[:, 6:8])
            ts(wsc_sb[:, 6:8], wsc_sb[:, 6:8], 1e-5, None, ALU.max)
            if DBG:
                vector.tensor_copy(st2[:, 18:20], wsc_sb[:, 6:8])
            vector.reciprocal(wsc_sb[:, 0:2], wsc_sb[:, 6:8])
            vector.reciprocal(wsc_sb[:, 2:4], wsc_sb[:, 0:2])
            ts(wsc_sb[:, 4:5], wsc_sb[:, 2:3], 1.0 / 127.0, None, ALU.mult)
            it = ts(wsc_sb[:, 5:6], wsc_sb[:, 3:4], 1.0 / 127.0, None,
                    ALU.mult)
            incd(it)
            ev("wsc_ready", "sV")

            def _v_wq_rounds():
                _wqprev = [None]
                for wi, (nch, width) in enumerate([(8, DHID), (32, D)]):
                    for ch in range(nch):
                        wts(vector, f"wq_aff_{wi}_{ch}",
                            None if _wqprev[0] is None else
                            "wqst_%d_%d" % _wqprev[0])
                        _wqprev[0] = (wi, ch)
                        vector.tensor_copy(
                            wbuf[ch % 2][:, 0:width].bitcast(I32),
                            wqtmp[:, 0:width].bitcast(I32))
                        ts(wqtmp[:, 0:width], wbuf[ch % 2][:, 0:width], MAGIC,
                           1.0, ALU.subtract, ALU.min)
                        it = ts(wqout[:, 0:width], wqtmp[:, 0:width], -1.0,
                                None, ALU.max)
                        inc(it, "sV")
                        if wi == 0:
                            for cc in range(W1RES):
                                it = vector.tensor_copy(
                                    w1res[:, (cc * 8 + ch) * 128:
                                          (cc * 8 + ch + 1) * 128],
                                    wqout[:, cc * 128:(cc + 1) * 128])
                            inc(it, "sV")
                        ev(f"wq_v_{wi}_{ch}", "sV")

            if STAGE == 2:
                _v_wq_rounds()
            if DBG and STAGE == 2:
                vector.tensor_copy(st2[:, 20:22], P_D[:, 16:18])
            if STAGE < 3:
                raise StopEmission
            for t in range(NT1B):
                wt(vector, f"1b_s_exp_{t}")
                e3 = e_sb[t % 2][:].rearrange("p (h d) -> p h d", d=64)
                vector.reduce_sum(st1[:, 0:16], e3, axis=AX.X)
                vector.reciprocal(st1[:, 16:32], st1[:, 0:16])
                ts(st2[:, 0:16], st1[:, 16:32], S_HEAD, None, ALU.mult)
                if t >= 2:
                    wt(vector, f"1b_qst_{t-2}")
                    wt(vector, f"1b_s_eq_{t-2}")
                for h in range(16):
                    it = ts(q16_sb[t % 2][:, h * 64:(h + 1) * 64],
                            e_sb[t % 2][:, h * 64:(h + 1) * 64],
                            st2[:, h:h + 1], None, ALU.mult)
                inc(it, "sV")
                ev(f"1b_q16_{t}", "sV")
            wt(vector, "1b_ctx_done")
            vdelay()
            vector.tensor_copy(ctx_sb[:, 0:512], P_A[0:65, 0:512])
            it = vector.tensor_copy(ctx_sb[:, 512:1024],
                                    P_A[0:65, 512:1024])
            inc(it, "sV")
            ev("v_ctx_ev", "sV")
            wts(vector, "g_qrows", "c_small")
            ts(srow1, srow1, par_sb[0:1, 0:1], None, ALU.mult)
            it = ts(srow2, srow2, par_sb[0:1, 1:2], None, ALU.mult)
            inc(it, "sV")
            ev("v_qstage", "sV")
            _v_wq_rounds()

            if STAGE < 4:
                raise StopEmission
            wt(vector, "p2_ld")
            vector.reciprocal(sfac_sb[:], sfac_sb[:])
            it = ts(sfac_sb[:], sfac_sb[:], S_HEAD, None, ALU.mult)
            incd(it)
            ev("p2_sfac", "sV")
            it = vector.tensor_copy(ctx16_sb, ctx_sb[0:64, :])
            inc(it, "sV")
            ev("p2_ctx16", "sV")
            wt(vector, "wqst_1_31")
            for h in range(H):
                wt(vector, f"p2_wo_{h}")
                it = vector.tensor_copy(
                    Wo16h[0:64, h * 1024:(h + 1) * 1024], wo32_sb[h % 2])
                inc(it, "sV")
                ev(f"p2_wo_v_{h}", "sV")
            for h in range(H):
                for jh in range(2):
                    wt(vector, f"p2_m_{h}_{jh}")
                    vdelay(2)
                    base = (h % 2) * 64
                    it = ts(M16[base:base + 64,
                                (h // 2) * 1024 + jh * 512:
                                (h // 2) * 1024 + jh * 512 + 512],
                            P_D[base:base + 64, jh * 512:jh * 512 + 512],
                            sfac_sb[base:base + 64, h // 2:h // 2 + 1],
                            None, ALU.mult)
                    inc(it, "sV")
                    ev(f"p2_mev_{h}_{jh}", "sV")
            ts(srow1, srow1, par_sb[0:1, 1:2], None, ALU.mult)
            it = vector.scalar_tensor_tensor(srow2, srow2,
                                             par_sb[0:1, 0:1], srow1,
                                             ALU.mult, ALU.add)
            inc(it, "sV")
            ev("p2_qh", "sV")
            wt(vector, "p2_s_stats")
            ts(st1[0:1, 16:17], st1[0:1, 2:3], 1.0 / D, None, ALU.mult)
            vector.tensor_mul(st1[0:1, 17:18], st1[0:1, 16:17],
                              st1[0:1, 16:17])
            it = ts(st1[0:1, 3:4], st1[0:1, 0:1], 1.0 / D, st1[0:1, 17:18],
                    ALU.mult, ALU.subtract)
            incd(it)
            ev("p2_v_var", "sV")
            wt(vector, "p2_s_rstd")
            vdelay(2)
            vector.tensor_mul(st1[0:1, 18:19], st1[0:1, 16:17],
                              st1[0:1, 5:6])
            ts(srow1, srow3, st1[0:1, 18:19], None, ALU.subtract)
            it = vector.scalar_tensor_tensor(srow3, P_B[0:1, :],
                                             st1[0:1, 5:6], srow1,
                                             ALU.mult, ALU.add)
            inc(it, "sV")
            ev("p2_v_x1h", "sV")
            wt(vector, "p2_s_stats2")
            ts(st1[0:1, 16:17], st1[0:1, 10:11], 1.0 / D, None, ALU.mult)
            vector.tensor_mul(st1[0:1, 17:18], st1[0:1, 16:17],
                              st1[0:1, 16:17])
            it = ts(st1[0:1, 11:12], st1[0:1, 8:9], 1.0 / D,
                    st1[0:1, 17:18], ALU.mult, ALU.subtract)
            incd(it)
            ev("p2_v_var2", "sV")
            wt(vector, "p2_s_rstd2")
            ts(srow1, srow3, st1[0:1, 16:17], st1[0:1, 13:14],
               ALU.subtract, ALU.mult)
            vector.reduce_max(st1[0:1, 20:21], srow1, axis=AX.X,
                              apply_absolute_value=True)
            ts(st1[0:1, 20:21], st1[0:1, 20:21], 1e-5, None, ALU.max)
            vector.reciprocal(st1[0:1, 21:22], st1[0:1, 20:21])
            ts(st1[0:1, 22:23], st1[0:1, 21:22], 127.0, None, ALU.mult)
            vector.tensor_mul(st1[0:1, 23:24], st1[0:1, 20:21],
                              wsc_sb[0:1, 4:5])
            ts(srow2, srow1, st1[0:1, 22:23], MAGIC, ALU.mult, ALU.add)
            vector.tensor_copy(srow1.bitcast(I32), srow2.bitcast(I32))
            it = ts(srow2, srow1, MAGIC, st1[0:1, 23:24],
                    ALU.subtract, ALU.mult)
            incd(it)
            ev("p2_v_xqh", "sV")
            ts(srow1, srow2, par_sb[0:1, 0:1], None, ALU.mult)
            it = ts(srow2, srow2, par_sb[0:1, 1:2], None, ALU.mult)
            inc(it, "sV")
            ev("p2_v_hmask", "sV")

            if STAGE < 5:
                raise StopEmission
            for phase_idx, tiles in enumerate(A_SCHED):
                for t in tiles:
                    psA = P_A if t % 2 == 0 else P_B
                    wt(vector, f"a_s_st_{t}")
                    vector.tensor_add(st1[:, 4:5], st1[:, 0:1], st1[:, 1:2])
                    vector.tensor_add(st1[:, 5:6], st1[:, 2:3], st1[:, 3:4])
                    ts(st1[:, 4:5], st1[:, 4:5], 1.0 / D, None, ALU.mult)
                    vector.tensor_mul(st1[:, 7:8], st1[:, 4:5], st1[:, 4:5])
                    it = ts(st1[:, 6:7], st1[:, 5:6], 1.0 / D, st1[:, 7:8],
                            ALU.mult, ALU.subtract)
                    incd(it)
                    ev(f"a_v_var_{t}", "sV")
                    wts(vector, f"a_s_rstd_{t}", f"a_ld_{t}")
                    vector.tensor_mul(st1[:, 10:11], st1[:, 4:5],
                                      st1[:, 9:10])
                    ts(scr2[:], x_sb[t % 2][:], st1[:, 10:11], None,
                       ALU.subtract)
                    it = vector.scalar_tensor_tensor(
                        x1_sb[t % 2][:], psA[:], st1[:, 9:10], scr2[:],
                        ALU.mult, ALU.add, accum_out=st2[:, 1:2])
                    inc(it, "sV")
                    ev(f"a_x1_{t}", "sV")
                    wt(vector, f"a_s_sq2_{t}")
                    ts(st2[:, 2:3], st2[:, 1:2], 1.0 / D, None, ALU.mult)
                    vector.tensor_mul(st2[:, 7:8], st2[:, 2:3], st2[:, 2:3])
                    it = ts(st2[:, 3:4], st2[:, 0:1], 1.0 / D, st2[:, 7:8],
                            ALU.mult, ALU.subtract)
                    incd(it)
                    ev(f"a_v_var2_{t}", "sV")
                    wt(vector, f"a_s_rstd2_{t}")
                    vector.tensor_mul(st2[:, 6:7], st2[:, 2:3], st2[:, 5:6])
                    it = ts(st2[:, 6:7], st2[:, 6:7], -1.0, None, ALU.mult)
                    incd(it)
                    ev(f"a_v_nm2_{t}", "sV")
                    wt(vector, f"a_s_xln2_{t}")
                    vector.reduce_max(st2[:, 9:10], scr2[:], axis=AX.X,
                                      apply_absolute_value=True)
                    ts(st2[:, 9:10], st2[:, 9:10], 1e-5, None, ALU.max)
                    vector.reciprocal(st2[:, 8:9], st2[:, 9:10])
                    ts(st2[:, 8:9], st2[:, 8:9], 127.0, None, ALU.mult)
                    vector.tensor_mul(st2[:, 10:11], st2[:, 9:10],
                                      wsc_sb[:, 4:5])
                    it = ts(st2[:, 11:12], st2[:, 10:11], -MAGIC, None,
                            ALU.mult)
                    incd(it)
                    ev(f"a_v_sc_{t}", "sV")
                    wt(vector, f"a_s_rt_{t}")
                    it = vector.tensor_copy(scrR[:].bitcast(I32),
                                            scrR[:].bitcast(I32))
                    inc(it, "sV")
                    ev(f"a_v_bc_{t}", "sV")
                if phase_idx == 0:
                    continue
                g = phase_idx - 1
                if STAGE < 6 or g >= WLIM:
                    continue
                W0 = g * GW
                ntt = _ceil(GOUT[g], 128)
                def _v_tree(ch):
                    wt(vector, f"w_abs_{g}_{ch}")
                    hw = usc_sb[ch % 2][:, 0:510]
                    if ch == 0:
                        it = vector.tensor_copy(amx_sb[:, 0:510], hw)
                    else:
                        it = vector.tensor_max(amx_sb[:, 0:510],
                                               amx_sb[:, 0:510], hw)
                    inc(it, "sV")
                    ev(f"w_tr_{g}_{ch}", "sV")

                for ch in range(32):
                    wt(vector, f"w_cv1_{g}_{ch}")
                    vector.scalar_tensor_tensor(
                        ysc_sb[ch % 2][:, 0:510],
                        harena[:, ch * 512 + 1:ch * 512 + 511],
                        cw_sb[:, 32 + ch:33 + ch], ysc_sb[ch % 2][:, 0:510],
                        ALU.mult, ALU.add)
                    it = vector.scalar_tensor_tensor(
                        ysc_sb[ch % 2][:, 0:510],
                        harena[:, ch * 512 + 2:ch * 512 + 512],
                        cw_sb[:, 64 + ch:65 + ch], ysc_sb[ch % 2][:, 0:510],
                        ALU.mult, ALU.add)
                    inc(it, "sV")
                    ev(f"w_cv3_{g}_{ch}", "sV")
                    if ch >= 1:
                        _v_tree(ch - 1)
                _v_tree(31)
                ev(f"w_tree_{g}", "sV")
                wt(vector, f"w_tp_{g}_3")
                vdelay()
                for tt in range(4):
                    vector.reduce_max(st1[:, 28 + tt:29 + tt],
                                      P_A[:, tt * 128:(tt + 1) * 128],
                                      axis=AX.X)
                ts(st1[:, 28:32], st1[:, 28:32], 1e-5, None, ALU.max)
                vector.reciprocal(st2[:, 24:28], st1[:, 28:32])
                ts(st2[:, 24:28], st2[:, 24:28], 127.0, None, ALU.mult)
                for tt in range(4):
                    it = vector.tensor_mul(st2[:, 28 + tt:29 + tt],
                                           st1[:, 28 + tt:29 + tt],
                                           wsc_sb[:, 5:6])
                incd(it)
                ev(f"w_sc_{g}", "sV")
                wt(vector, f"w_srow_pe_{g}")
                vdelay(2)
                it = vector.tensor_copy(srow_sb[:], P_B[0:1, 0:512])
                inc(it, "sV")
                ev(f"w_srow_{g}", "sV")
                wt(vector, f"w_srep_pe_{g}")
                vdelay(2)
                it = vector.tensor_copy(srep_sb, P_B[:, 512:1024])
                inc(it, "sV")
                ev(f"w_srep_{g}", "sV")
                for tt in range(ntt):
                    pr = min(128, GOUT[g] - tt * 128)
                    pst = PT[tt]
                    wt(vector, f"w_s_fst_{g}_{tt}")
                    vector.tensor_add(st1[0:pr, 20:21], st1[0:pr, 16:17],
                                      st1[0:pr, 17:18])
                    vector.tensor_add(st1[0:pr, 21:22], st1[0:pr, 18:19],
                                      st1[0:pr, 19:20])
                    ts(st1[0:pr, 20:21], st1[0:pr, 20:21], 1.0 / D, None,
                       ALU.mult)
                    vector.tensor_mul(st1[0:pr, 22:23], st1[0:pr, 20:21],
                                      st1[0:pr, 20:21])
                    ts(st1[0:pr, 21:22], st1[0:pr, 21:22], 1.0 / D,
                       st1[0:pr, 22:23], ALU.mult, ALU.subtract)
                    vector.tensor_mul(st1[0:pr, 22:23],
                                      st2[0:pr, 28 + tt:29 + tt],
                                      st2[0:pr, 28 + tt:29 + tt])
                    it = vector.tensor_mul(st1[0:pr, 23:24],
                                           st1[0:pr, 21:22],
                                           st1[0:pr, 22:23])
                    incd(it)
                    ev(f"w_v_fvar_{g}_{tt}", "sV")
                    wt(vector, f"w_s_frstd_{g}_{tt}")
                    vector.tensor_mul(st1[0:pr, 26:27],
                                      st2[0:pr, 28 + tt:29 + tt],
                                      st1[0:pr, 25:26])
                    vector.tensor_mul(st1[0:pr, 27:28], st1[0:pr, 20:21],
                                      st1[0:pr, 26:27])
                    wt(vector, f"w_x1r_{g}_{tt}")
                    ts(scrW[0:pr, :], x1r_sb[tt % 2][0:pr, :],
                       st1[0:pr, 27:28], None, ALU.subtract)
                    it = vector.scalar_tensor_tensor(
                        x1r_sb[tt % 2][0:pr, :], pst[0:pr, :],
                        st1[0:pr, 26:27], scrW[0:pr, :], ALU.mult, ALU.add)
                    inc(it, "sV")
                    ev(f"w_out_{g}_{tt}", "sV")
            ev("vector_done", "sV")

        blk.vector(_stage_guard(_vector_body, fin=_fin_vector))

        def _gp_body(gp):
            sdma = mkdma(gp)

            sdma("gbq", srow3, bq_d[:, :])
            ev("g_bq", "gbq")
            sdma("gbq", id32[:], ident_d[:, :], gate=True)
            ev("g_id", "gbq")


            if STAGE < 3:
                raise StopEmission
            wts(gp, f"1b_qst_{NT1B-1}", f"1b_q16_{NT1B-1}")
            sdma("gqr", srow1, q_hbm[0:1, :])
            _lq = NT1B * 128 - 1
            sdma("gqr", srow2, q_hbm[_lq:_lq + 1, :])
            ev("g_qrows", "gqr")
            wt(gp, "v_ctx_ev")
            sdma("gcc", cc_in[0:65, :], ctx_sb)
            wt(gp, "v_qstage")
            gp.wait_ge(sem["gqr"], cnt["gqr"])
            sdma("gcc", cc_in[65:66, :], srow1)
            sdma("gcc", cc_in[66:67, :], srow2)
            if STAGE < 4:
                raise StopEmission
            gp.wait_ge(sem["gcc"], cnt["gcc"])
            it = gp.collective_compute(
                "AllReduce", ALU.add,
                replica_groups=[[0, 1], [2, 3], [4, 5], [6, 7]],
                ins=[cc_in.ap().opt()],
                outs=[cc_out.ap().opt()],
            )
            it.then_inc(sCC, 1)
            gp.wait_ge(sCC, 1)

            wt(gp, "p2_qh")
            sdma("gh", hrow_hbm[:, :], srow2)
            gp.wait_ge(sem["gh"], cnt["gh"])
            sdma("gh", qh16[:, 0:8],
                 hrow_hbm[0:1, :].rearrange("r (g p) -> p (r g)", p=128),
                 allow_slow_non_contiguous=True)
            ev("p2_qhT", "gh")
            wt(gp, "p2_v_hmask")
            gp.wait_ge(sem["gh"], cnt["gh"])
            sdma("gh", xq_hbm[0:1, :], srow1)
            sdma("gh", xq_hbm[TOK + 1:TOK + 2, :], srow2)
            it = gp.memset(scrW[:, 0:1024], 0.0)
            inc(it, "sG")
            ev("g_pad0", "sG")
            wt(gp, "g_pad0")
            r = TOK + 2
            while r < XQW:
                nr = min(128, XQW - r)
                sdma("gh", xq_hbm[r:r + nr, :], scrW[0:nr, 0:1024])
                r += nr
            ev("g_pad", "gh")

            if STAGE < 6:
                raise StopEmission
            for g in range(min(NG, WLIM)):
                for ch in range(32):
                    if ch >= W1RES:
                        w1s = "gw1a" if ch % 2 == 0 else "gw1b"
                        if g == 0:
                            wt(gp, "wqst_0_7")
                        if ch >= W1RES + 2:
                            wt(gp, f"w_pe_h_{g}_{ch-2}")
                        elif g > 0:
                            wt(gp, f"w_pe_h_{g-1}_31")
                        sdma(w1s, w1st[ch % 2][:],
                             wq1_hbm[ch - W1RES].rearrange("p k c -> p (k c)"))
                        ev(f"w1s_{g}_{ch}", w1s)
                def _g_u(ch):
                    wt(gp, f"w_silu_{g}_{ch}")
                    if ch >= 2:
                        wt(gp, f"w_rt_{g}_{ch-2}")
                    it = gp.tensor_tensor(
                        usc_sb[ch % 2][:, 0:510],
                        harena[:, ch * 512 + 1:ch * 512 + 511],
                        srep_sb[:, 0:510], ALU.mult)
                    inc(it, "sG")
                    ev(f"w_u_{g}_{ch}", "sG")

                def _g_bc(ch):
                    wt(gp, f"w_rt_{g}_{ch}")
                    it = gp.tensor_copy(
                        ysc_sb[ch % 2][:, 0:510].bitcast(I32),
                        ysc_sb[ch % 2][:, 0:510].bitcast(I32))
                    inc(it, "sG")
                    ev(f"w_bc_{g}_{ch}", "sG")

                wt(gp, f"w_srep_{g}")
                _g_u(0)
                for ch in range(32):
                    if ch + 1 < 32:
                        _g_u(ch + 1)
                    _g_bc(ch)

        blk.gpsimd(_stage_guard(_gp_body, fin=_fin_gp))

        def _tensor_body(tensor):
            def mm(out, lhsT, rhs, **kw):
                return tensor.matmul(out, lhsT, rhs, **kw)

            if STAGE < 2:
                raise StopEmission
            wt(tensor, "v_const")
            wt(tensor, "wmean_all")
            it = mm(P_D[:, 16:18], allones[:], wst[:, 2:4],
                    start=True, stop=True)
            inc(it, "sP")
            ev("w_srep_pe", "sP")

            if STAGE < 3:
                raise StopEmission
            wt(tensor, "v_bias")
            for t in range(NT1B):
                wt(tensor, f"1b_xlt_{t}")
                if t >= 2:
                    wt(tensor, f"1b_s_exp_{t-2}")
                if t == 1:
                    wt(tensor, "wsc_ready")
                PQB = P_B if t % 2 == 0 else P_D
                for jh in range(2):
                    for gch in range(8):
                        mm(PQB[:, jh * 512:(jh + 1) * 512],
                           xlt_sb[t % 2][:, gch * 128:(gch + 1) * 128],
                           Wq16[:, gch * 1024 + jh * 512:
                                gch * 1024 + jh * 512 + 512],
                           start=(gch == 0), stop=False)
                    it = mm(PQB[:, jh * 512:(jh + 1) * 512], ones16[0:1, :],
                            bqv16[0:1, jh * 512:jh * 512 + 512],
                            start=False, stop=True)
                inc(it, "sP")
                ev(f"1b_pe_q_{t}", "sP")
                if t >= 1:
                    wt(tensor, f"1b_s_v_{t-1}")
                for jh in range(2):
                    for gch in range(8):
                        mm(P_C[:, jh * 512:(jh + 1) * 512],
                           xlt_sb[t % 2][:, gch * 128:(gch + 1) * 128],
                           Wv16[:, gch * 1024 + jh * 512:
                                gch * 1024 + jh * 512 + 512],
                           start=(gch == 0), stop=False)
                    it = mm(P_C[:, jh * 512:(jh + 1) * 512], ones16[0:1, :],
                            bqv16[0:1, 1024 + jh * 512:1024 + jh * 512 + 512],
                            start=False, stop=True)
                inc(it, "sP")
                ev(f"1b_pe_vmm_{t}", "sP")
                if t >= 1:
                    wt(tensor, f"1b_s_eq_{t-1}")
                    va = vaug_sb[(t - 1) % 2]
                    for h in range(16):
                        it = mm(P_A[0:65, h * 64:(h + 1) * 64],
                                va[:, h * 68:h * 68 + 65],
                                eq_sb[(t - 1) % 2][:, h * 64:(h + 1) * 64],
                                start=(t - 1 == 0 and h % 8 == 0),
                                stop=False)
                    inc(it, "sP")
                ev(f"1b_pe_v_{t}", "sP")
            wt(tensor, f"1b_s_eq_{NT1B-1}")
            va = vaug_sb[(NT1B - 1) % 2]
            for h in range(16):
                it = mm(P_A[0:65, h * 64:(h + 1) * 64],
                        va[:, h * 68:h * 68 + 65],
                        eq_sb[(NT1B - 1) % 2][:, h * 64:(h + 1) * 64],
                        start=(NT1B == 1 and h % 8 == 0),
                        stop=(h % 8 == 7))
            inc(it, "sP")
            ev("1b_ctx_done", "sP")

            if STAGE < 4:
                raise StopEmission
            wt(tensor, "p2_ctx16")
            wt(tensor, "p2_sfac")
            for h in range(H):
                wt(tensor, f"p2_wo_v_{h}")
                base = (h % 2) * 64
                for jh in range(2):
                    kw = {}
                    if base:
                        kw["tile_position"] = (0, 64)
                    it = mm(P_D[base:base + 64, jh * 512:jh * 512 + 512],
                            ctx16_sb[:, h * 64:(h + 1) * 64],
                            Wo16h[0:64, h * 1024 + jh * 512:
                                  h * 1024 + jh * 512 + 512],
                            start=True, stop=True, **kw)
                    inc(it, "sP")
                    ev(f"p2_m_{h}_{jh}", "sP")
                    wt(tensor, f"p2_mev_{h}_{jh}")
            wt(tensor, "p2_qhT")
            for jh in range(2):
                for g8 in range(8):
                    mm(P_B[0:1, jh * 512:jh * 512 + 512],
                       qh16[:, g8:g8 + 1],
                       M16[:, g8 * 1024 + jh * 512:g8 * 1024 + jh * 512 + 512],
                       start=(g8 == 0), stop=False)
                it = mm(P_B[0:1, jh * 512:jh * 512 + 512], ones16[0:1, 0:1],
                        bqv16[0:1, 2048 + jh * 512:2048 + jh * 512 + 512],
                        start=False, stop=True)
            inc(it, "sP")
            ev("p2_halo_pe", "sP")

            if STAGE < 5:
                raise StopEmission
            for phase_idx, tiles in enumerate(A_SCHED):
                for t in tiles:
                    psA = P_A if t % 2 == 0 else P_B
                    names = [f"a_ld_{t}", f"a_qt_{t}"]
                    if t >= 2:
                        names += [f"a_s_st_{t-2}", f"a_x1_{t-2}"]
                    if phase_idx == 0 and t == 0:
                        names.append("v_ctx_ev")
                    if t % 2 == 1 and phase_idx == 0 and t == 1:
                        names += ["p2_s_stats2", "p2_v_x1h"]
                    if phase_idx >= 2:
                        gprev = phase_idx - 2
                        names.append(
                            f"w_out_{gprev}_{_ceil(GOUT[gprev], 128) - 1}")
                    wts(tensor, *names)
                    for jh in range(2):
                        for g8 in range(8):
                            mm(psA[:, jh * 512:(jh + 1) * 512],
                               qt_sb[t % 2][:, g8 * 128:(g8 + 1) * 128],
                               M16[:, g8 * 1024 + jh * 512:
                                   g8 * 1024 + jh * 512 + 512],
                               start=(g8 == 0), stop=False)
                        it = mm(psA[:, jh * 512:(jh + 1) * 512],
                                ones16[0:1, :],
                                bqv16[0:1, 2048 + jh * 512:
                                      2048 + jh * 512 + 512],
                                start=False, stop=True)
                    inc(it, "sP")
                    ev(f"a_pe_{t}", "sP")
                if phase_idx == 0:
                    continue
                g = phase_idx - 1
                if STAGE < 6 or g >= WLIM:
                    continue
                W0 = g * GW
                ntt = _ceil(GOUT[g], 128)
                wt(tensor, f"w_xqT_{g}")
                for ch in range(32):
                    qtr = PQ[ch % 4]
                    names = []
                    if ch >= 4:
                        names.append(f"w_hev_{g}_{ch-4}")
                    elif g > 0:
                        names.append(f"w_hev_{g-1}_{28 + ch}")
                        if ch >= 2 and _ceil(GOUT[g - 1], 128) > ch:
                            names.append(f"w_out_{g-1}_{ch}")
                    if ch < W1RES:
                        lsrc = [w1res[:, (ch * 8 + k) * 128:
                                      (ch * 8 + k + 1) * 128]
                                for k in range(8)]
                        names.append("wq_v_0_7")
                    else:
                        names.append(f"w1s_{g}_{ch}")
                        lsrc = [w1st[ch % 2][:, k * 128:(k + 1) * 128]
                                for k in range(8)]
                    wts(tensor, *names)
                    for k in range(8):
                        rsrc = (xqTw if k < 4 else xqTw2)
                        it = mm(qtr, lsrc[k],
                                rsrc[:, (k % 4) * 512:(k % 4 + 1) * 512],
                                start=(k == 0), stop=(k == 7))
                    inc(it, "sP")
                    ev(f"w_pe_h_{g}_{ch}", "sP")
                ev(f"w_w1done_{g}", "sP")
                wt(tensor, f"w_tree_{g}")
                wt(tensor, "g_id")
                wt(tensor, f"a_x1_{tiles[-1]}" if tiles else None)
                for tt in range(4):
                    it = tensor.transpose(P_A[:, tt * 128:(tt + 1) * 128],
                                          amx_sb[:, tt * 128:(tt + 1) * 128],
                                          id32[:])
                    inc(it, "sP")
                    ev(f"w_tp_{g}_{tt}", "sP")
                wt(tensor, f"w_sc_{g}")
                for tt in range(4):
                    it = mm(P_B[0:1, tt * 128:(tt + 1) * 128],
                            st2[:, 24 + tt:25 + tt], id32[:],
                            start=(tt == 0), stop=(tt == 3))
                inc(it, "sP")
                ev(f"w_srow_pe_{g}", "sP")
                wt(tensor, f"w_srow_{g}")
                it = mm(P_B[:, 512:1024], ones32r[0:1, :], srow_sb[:],
                        start=True, stop=True)
                inc(it, "sP")
                ev(f"w_srep_pe_{g}", "sP")
                for ch in range(32):
                    names = [f"w2s_{g}_{ch}", f"w_hq_{g}_{ch}"]
                    if ch == 0:
                        if g:
                            names.append(
                                f"w_s_fst_{g-1}_{_ceil(GOUT[g-1], 128) - 1}")
                        names.append(f"w_srep_{g}")
                    wts(tensor, *names)
                    for tt in range(ntt):
                        pr = min(128, GOUT[g] - tt * 128)
                        pst = PT[tt]
                        lh = ha_bf[:, ch * 1024 + 1 + tt * 128:
                                   ch * 1024 + 1 + tt * 128 + pr]
                        for jh in range(2):
                            it = mm(pst[0:pr, jh * 512:(jh + 1) * 512],
                                    lh, w2st[ch % 2][:, jh * 512:
                                                     (jh + 1) * 512],
                                    start=(ch == 0), stop=(ch == 31))
                    inc(it, "sP")
                    ev(f"w_pe_w2_{g}_{ch}", "sP")
                ev(f"w_pe_f_{g}", "sP")

        blk.tensor(_stage_guard(_tensor_body, fin=_fin_tensor))

        if plan:
            for _sn in _DSTREAMS:
                tick["__tot_" + _sn] = (_sn, cnt[_sn])

    return nc


def _build():
    tick = {}
    build_nc(tick, plan=True)
    nc = build_nc(dict(tick), plan=False)
    return nc


def _host_in_maps(inputs):
    x = np.ascontiguousarray(np.asarray(inputs["L0_attn_x"],
                                        dtype=np.float32)).reshape(B * N, D)
    WqA = np.ascontiguousarray(inputs["L0_attn_Wq"], dtype=np.float32)
    WvA = np.ascontiguousarray(inputs["L0_attn_Wv"], dtype=np.float32)
    WoA = np.ascontiguousarray(inputs["L0_attn_Wo"], dtype=np.float32)
    bqA = np.asarray(inputs["L0_attn_bq"], dtype=np.float32).reshape(1, D)
    bvA = np.asarray(inputs["L0_attn_bv"], dtype=np.float32).reshape(1, D)
    boA = np.asarray(inputs["L0_attn_bo"], dtype=np.float32).reshape(1, D)
    w1TA = np.ascontiguousarray(
        np.asarray(inputs["L0_ffn_w1"], dtype=np.float32).T)
    w2TA = np.ascontiguousarray(
        np.asarray(inputs["L0_ffn_w2"], dtype=np.float32).T)
    cwA = np.ascontiguousarray(
        np.asarray(inputs["L0_ffn_conv_w"], dtype=np.float32)[:, 0, :]
        .reshape(DHID, 3).T)
    cbA = np.asarray(inputs["L0_ffn_conv_b"], dtype=np.float32).reshape(
        1, DHID)

    in_maps = []
    for c in range(NCORE):
        lo = c * TOK
        odd = c % 2
        halo_row = lo + TOK if odd == 0 else lo - 1
        halo = (x[halo_row:halo_row + 1]
                if 0 <= halo_row < B * N else np.zeros((1, D), np.float32))
        in_maps.append({
            "x_in": x[lo:lo + TOK],
            "x_halo": np.ascontiguousarray(halo),
            "Wq": WqA, "Wv": WvA, "Wo": WoA,
            "bq": bqA, "bv": bvA, "bo": boA,
            "w1T": w1TA, "w2T": w2TA, "cw": cwA, "cb": cbA,
            "par": np.array([[1.0 if odd else 0.0,
                              0.0 if odd else 1.0]], np.float32),
        })
    return in_maps


def run_raw(inputs):
    nc = _build()
    in_maps = _host_in_maps(inputs)
    return run_bass_kernel_spmd(nc, in_maps, core_ids=list(range(NCORE)))


def kernel(**inputs):
    res = run_raw(inputs)
    out = np.concatenate([np.asarray(r["y_out"]) for r in res.results],
                         axis=0)
    return out.reshape(B, N, D).astype(np.float32)
